# version 11
# baseline (speedup 1.0000x reference)
import numpy as np
from contextlib import ExitStack
import concourse.bass as bass
import concourse.mybir as mybir
from concourse.bass_utils import run_bass_kernel_spmd

F32 = mybir.dt.float32
F32R = mybir.dt.float32r
AF = mybir.ActivationFunctionType
OP = mybir.AluOpType

NCORE = 8
BC = 2
T = 2048
KD = 8
KI = 16
NST = 16
TCW = 512
NTC = T // TCW
G = 2
NGRP = KI // G
EPS = 1e-5


class Sem:
    def __init__(self, st, nc, name):
        self.h = st.enter_context(nc.semaphore(name))
        self.n = 0

    def inc(self, bi, amt=1):
        bi.then_inc(self.h, amt)
        self.n += amt
        return self.n


class SemGroup:
    def __init__(self, st, nc, name, k):
        self.sems = [Sem(st, nc, f"{name}{i}") for i in range(k)]
        self.i = 0

    def nxt(self):
        s = self.sems[self.i]
        self.i = (self.i + 1) % len(self.sems)
        return s


class Buf:
    CUR = ['?']
    WAITED = {}

    @classmethod
    def _wait(cls, eng, s, t):
        key = (cls.CUR[0], id(s))
        if cls.WAITED.get(key, -1) >= t:
            return
        eng.wait_ge(s.h, t)
        cls.WAITED[key] = t

    def __init__(self, ap):
        self.ap = ap
        self.ready = {}
        self.cons = {}

    def acquire(self, eng):
        for s, t in self.cons.items():
            Buf._wait(eng, s, t)
        self.cons = {}
        self.ready = {}

    def produced(self, s, tok):
        if s in self.ready:
            self.ready[s] = max(self.ready[s], tok)
        else:
            self.ready[s] = tok

    def wait_ready(self, eng):
        for s, t in self.ready.items():
            Buf._wait(eng, s, t)

    def consumed(self, s, tok):
        if s in self.cons:
            self.cons[s] = max(self.cons[s], tok)
        else:
            self.cons[s] = tok


def build_nc():
    nc = bass.Bass("TRN2", target_bir_lowering=False)

    def din(name, shape):
        return nc.dram_tensor(name, shape, F32, kind="ExternalInput").ap()

    def dout(name, shape):
        return nc.dram_tensor(name, shape, F32, kind="ExternalOutput").ap()

    import os as _os
    _DBG = _os.environ.get("KD_DBG", "0") == "1"

    def dint(name, shape):
        kind = "ExternalOutput" if _DBG else "Internal"
        return nc.dram_tensor(name, shape, F32, kind=kind).ap()

    xfm = din("xfm", [BC, KD, 128, T])
    W = {}
    for l in (0, 1):
        W[f"inwT{l}"] = din(f"inwT{l}", [32, 128, 8 * 128])
        W[f"outwT{l}"] = din(f"outwT{l}", [8, 128, 16 * 128])
        W[f"dtwT{l}"] = din(f"dtwT{l}", [64, 2048])
        W[f"xpwT{l}"] = din(f"xpwT{l}", [128, 16 * 96])
        W[f"convw{l}"] = din(f"convw{l}", [128, 64])
        W[f"convb{l}"] = din(f"convb{l}", [128, 16])
        W[f"dtb{l}"] = din(f"dtb{l}", [128, 16])
        W[f"dsk{l}"] = din(f"dsk{l}", [128, 16])
        W[f"alog{l}"] = din(f"alog{l}", [128, 256])
        W[f"lnw{l}"] = din(f"lnw{l}", [128, 8])
        W[f"lnb{l}"] = din(f"lnb{l}", [128, 8])
    ffnw = din("ffnw", [8, 128, 1024])
    ffna1 = din("ffna1", [8, 128, 1024])
    ffna2 = din("ffna2", [8, 128, 1024])
    ffnb = din("ffnb", [128, 8])
    onesr = din("onesr", [1, 128])
    onesc = din("onesc", [128, 1])
    cst = din("cst", [128, 4])

    outfm = dout("outfm", [BC, KD, 128, T])

    xn_d = dint("xn_d", [BC, KD, 128, T])
    ffn_d = dint("ffn_d", [BC, KD, 128, T])
    h1_d = dint("h1_d", [BC, KD, 128, T])
    h2_d = dint("h2_d", [BC, KD, 128, T])
    m1_d = dint("m1_d", [BC, KD, 128, T])
    m2_d = dint("m2_d", [BC, KD, 128, T])
    u_d = dint("u_d", [BC, KI, 128, T])
    xdbl_d = nc.dram_tensor("xdbl_d", [BC, 96, T], F32R,
                            kind="ExternalOutput" if _DBG else "Internal").ap()
    mu_d = nc.dram_tensor("mu_d", [BC, 2, T], F32R,
                          kind="ExternalOutput" if _DBG else "Internal").ap()
    zs_d = dint("zs_d", [BC, KI, 128, T])
    yp_d = dint("yp_d", [BC, KI, 128, T])

    st = ExitStack()
    st.__enter__()

    def mksb(name, shape, dt=F32):
        return Buf(st.enter_context(nc.sbuf_tensor(name, shape, dt)).ap())

    NPOOL = 13
    pool = [mksb(f"pl{i}", [128, T]) for i in range(NPOOL)]
    conv_sb = pool[11]
    xdbl = [mksb(f"xdbl{b}", [96, T], F32R) for b in range(BC)]
    wt_in = [mksb("wt_in0", [128, 1024]), mksb("wt_in1", [128, 1024])]
    wt_f = [mksb("wt_f0", [128, 1024], F32R), mksb("wt_f1", [128, 1024], F32R)]
    ld_r = [mksb("ld_r0", [128, T], F32R), mksb("ld_r1", [128, T], F32R)]
    wt_out = [mksb("wt_out0", [128, 2048], F32R), mksb("wt_out1", [128, 2048], F32R)]
    xpwT_sb = mksb("xpwT_sb", [128, 1536], F32R)
    dtw1 = mksb("dtw1", [64, 128], F32R)
    hist_sb = mksb("hist_sb", [128, 64])
    u_r = mksb("u_r", [128, T], F32R)
    onesr_r = mksb("onesr_r", [1, 128], F32R)
    A_sb = mksb("A_sb", [128, 256])
    convw_sb = mksb("convw_sb", [128, 64])
    convb_sb = mksb("convb_sb", [128, 16])
    dtb_sb = mksb("dtb_sb", [128, 16])
    dsk_sb = mksb("dsk_sb", [128, 16])
    lnw_sb = mksb("lnw_sb", [128, 8])
    lnb_sb = mksb("lnb_sb", [128, 8])
    ffnb_sb = mksb("ffnb_sb", [128, 8])
    onesc_sb = mksb("onesc_sb", [128, 1])
    cst_sb = mksb("cst_sb", [128, 4])
    stgB = mksb("stgB", [1, T], F32R)
    stgC = mksb("stgC", [1, T], F32R)
    srr = stgC

    ps_a = Buf(st.enter_context(nc.psum_tensor("ps_a", [128, T], F32)).ap())
    ps_b = [Buf(st.enter_context(nc.psum_tensor(f"ps_b{j}", [128, TCW], F32)).ap())
            for j in range(NTC)]

    g_ld = SemGroup(st, nc, "s_ld", 4)
    g_st = SemGroup(st, nc, "s_st", 4)
    g_sc = SemGroup(st, nc, "s_sc", 2)
    s_g = Sem(st, nc, "s_g")
    s_t = Sem(st, nc, "s_t")
    s_v = Sem(st, nc, "s_v")
    s_a = Sem(st, nc, "s_a")

    block = st.enter_context(nc.Block())

    Buf.WAITED = {}

    class RecInstr:
        __slots__ = ("incs",)
        def __init__(self):
            self.incs = []
        def then_inc(self, sem, amt):
            self.incs.append((sem, amt))
            return self

    class RecEng:
        __slots__ = ("lst", "kind")
        def __init__(self, lst, kind):
            self.lst = lst
            self.kind = kind
        def __getattr__(self, name):
            lst = self.lst
            kind = self.kind
            def call(*a, **kw):
                ri = RecInstr()
                lst.append((name, a, kw, ri))
                trace_log.append((kind, name, a, kw, ri))
                return ri
            return call

    rec = {k: [] for k in 'tvalg'}
    trace_log = []

    _eng_of = {}

    def flush():
        _eng_of.setdefault('t', block.tensor)
        for k, emit in (('t', block.tensor), ('v', block.vector),
                        ('a', block.scalar), ('l', block.sync),
                        ('g', block.gpsimd)):
            lst = rec[k]
            if not lst:
                continue
            rec[k] = []
            def f(e, lst=lst):
                for name, a, kw, ri in lst:
                    bi = getattr(e, name)(*a, **kw)
                    for sem, amt in ri.incs:
                        bi.then_inc(sem, amt)
            emit(f)

    def T_(fn):
        Buf.CUR[0] = 't'
        fn(RecEng(rec['t'], 't'))

    def V_(fn):
        Buf.CUR[0] = 'v'
        fn(RecEng(rec['v'], 'v'))

    def A_(fn):
        Buf.CUR[0] = 'a'
        fn(RecEng(rec['a'], 'a'))

    def L_(fn):
        Buf.CUR[0] = 'l'
        fn(RecEng(rec['l'], 'l'))

    def G_(fn):
        Buf.CUR[0] = 'g'
        fn(RecEng(rec['g'], 'g'))

    import os
    _PIN = os.environ.get("KD_PIN", "0") == "1"
    _GADD = os.environ.get("KD_GADD", "1") == "1"

    def pin_act(idx):
        if not _PIN:
            return
        def f(e):
            e.add_instruction(mybir.InstLoadActFuncSet(
                name=nc.get_next_instruction_name(),
                act_func_set_id=idx, ins=[], outs=[]))
        A_(f)

    def load(buf, src_ap, dst_ap=None):
        def f(e):
            buf.acquire(e)
            bi = e.dma_start(out=dst_ap if dst_ap is not None else buf.ap,
                             in_=src_ap)
            s = g_ld.nxt()
            buf.produced(s, s.inc(bi, 16))
        L_(f)

    def loadg(buf, src_ap, dst_ap=None):
        def f(e):
            buf.acquire(e)
            bi = e.dma_start(out=dst_ap if dst_ap is not None else buf.ap,
                             in_=src_ap)
            s = g_st.nxt()
            buf.produced(s, s.inc(bi, 16))
        G_(f)

    def store(buf, dst_ap, src_ap=None, key=None):
        def f(e):
            buf.wait_ready(e)
            bi = e.dma_start(out=dst_ap,
                             in_=src_ap if src_ap is not None else buf.ap)
            s = g_st.nxt()
            tok = s.inc(bi, 16)
            buf.consumed(s, tok)
            if key is not None:
                store_toks[key] = (s, tok)
        G_(f)

    store_toks = {}

    def loads_barrier():
        tgts = [(s, s.n) for s in g_st.sems]
        def fl(e):
            for s, t in tgts:
                Buf._wait(e, s, t)
        L_(fl)
        def fg(e):
            for s, t in tgts:
                Buf._wait(e, s, t)
        G_(fg)

    load(onesc_sb, onesc)
    load(ffnb_sb, ffnb)
    load(cst_sb, cst)
    loadg(onesr_r, onesr)

    def ln_stats_and_bcast(tiles, sq_slot, st_slot):
        st1 = st_slot
        for k in range(KD):
            def fa(e, k=k):
                sq_slot.acquire(e)
                tiles[k].wait_ready(e)
                bi = e.activation(sq_slot.ap, tiles[k].ap, AF.Square)
                tok = s_a.inc(bi)
                sq_slot.produced(s_a, tok)
                tiles[k].consumed(s_a, tok)
            A_(fa)

            def ft(e, k=k):
                tiles[k].wait_ready(e)
                sq_slot.wait_ready(e)
                if k == 0:
                    ps_a.acquire(e)
                    for j in range(NTC):
                        ps_b[j].acquire(e)
                tok = None
                for j in range(NTC):
                    e.matmul(ps_a.ap[0:1, j * TCW:(j + 1) * TCW],
                             onesc_sb.ap, tiles[k].ap[:, j * TCW:(j + 1) * TCW],
                             start=(k == 0), stop=(k == KD - 1))
                    bi2 = e.matmul(ps_b[j].ap[0:1, :],
                                   onesc_sb.ap, sq_slot.ap[:, j * TCW:(j + 1) * TCW],
                                   start=(k == 0), stop=(k == KD - 1))
                    tok = s_t.inc(bi2)
                sq_slot.consumed(s_t, tok)
                tiles[k].consumed(s_t, tok)
                if k == KD - 1:
                    ps_a.produced(s_t, tok)
                    for j in range(NTC):
                        ps_b[j].produced(s_t, tok)
            T_(ft)

        def fv(e):
            ps_a.wait_ready(e)
            for j in range(NTC):
                ps_b[j].wait_ready(e)
            ld_r[0].acquire(e)
            smu_ap = ld_r[0].ap[0:1, :]
            st1.acquire(e)
            pool[12].acquire(e)
            mu2 = pool[12].ap[0:1, :]
            s1 = st1.ap[0:1, :]
            e.tensor_scalar(smu_ap, ps_a.ap[0:1, :], 1.0 / 1024.0, None, OP.mult)
            e.tensor_scalar(mu2, ps_a.ap[0:1, :], 1.0 / 1024.0, None, OP.mult)
            for j in range(NTC):
                e.tensor_scalar(s1[:, j * TCW:(j + 1) * TCW],
                                ps_b[j].ap[0:1, :], 1.0 / 1024.0, None, OP.mult)
            e.tensor_mul(mu2, mu2, mu2)
            bi = e.tensor_sub(s1, s1, mu2)
            tok = s_v.inc(bi)
            ps_a.consumed(s_v, tok)
            for j in range(NTC):
                ps_b[j].consumed(s_v, tok)
            pool[12].consumed(s_v, tok)
            ld_r[0].produced(s_v, tok)
            st1.produced(s_v, tok)
        V_(fv)

        def fa2(e):
            st1.wait_ready(e)
            cst_sb.wait_ready(e)
            s1 = st1.ap[0:1, :]
            e.activation(s1, s1, AF.Ln, bias=cst_sb.ap[0:1, 0:1])
            srr.acquire(e)
            bi = e.activation(srr.ap, s1, AF.Exp,
                              scale=cst_sb.ap[0:1, 2:3])
            tok = s_a.inc(bi)
            st1.consumed(s_a, tok)
            srr.produced(s_a, tok)
        A_(fa2)

        def ft2(e):
            ps_a.acquire(e)
            ld_r[0].wait_ready(e)
            smu_ap = ld_r[0].ap[0:1, :]
            srr.wait_ready(e)
            onesr_r.wait_ready(e)
            tok = None
            for j in range(NTC):
                ps_b[j].acquire(e)
                e.matmul(ps_a.ap[:, j * TCW:(j + 1) * TCW], onesr_r.ap,
                         smu_ap[:, j * TCW:(j + 1) * TCW], start=True, stop=True)
                bi2 = e.matmul(ps_b[j].ap, onesr_r.ap,
                               srr.ap[:, j * TCW:(j + 1) * TCW], start=True, stop=True)
                tok = s_t.inc(bi2)
            ld_r[0].consumed(s_t, tok)
            srr.consumed(s_t, tok)
            ps_a.produced(s_t, tok)
            for j in range(NTC):
                ps_b[j].produced(s_t, tok)
        T_(ft2)

    def p0_ln(b):
        for k in range(KD):
            load(pool[k], xfm[b, k])
        yield
        for phase in (0, 1):
            for jh in (0, 1):
                for k in range(KD):
                    if phase == 1:
                        def fsq(e, k=k):
                            pool[11].acquire(e)
                            pool[k].wait_ready(e)
                            bi = e.activation(pool[11].ap, pool[k].ap,
                                              AF.Square)
                            tok = s_a.inc(bi)
                            pool[11].produced(s_a, tok)
                            pool[k].consumed(s_a, tok)
                        A_(fsq)

                    def fst(e, k=k, jh=jh, phase=phase):
                        srcb = pool[k] if phase == 0 else pool[11]
                        srcb.wait_ready(e)
                        tok = None
                        for jj in (0, 1):
                            j = 2 * jh + jj
                            bank = ps_b[2 + jj]
                            if k == 0:
                                bank.acquire(e)
                            bi = e.matmul(bank.ap[0:1, :], onesc_sb.ap,
                                          srcb.ap[:, j * TCW:(j + 1) * TCW],
                                          start=(k == 0), stop=(k == KD - 1))
                            tok = s_t.inc(bi)
                        srcb.consumed(s_t, tok)
                        if k == KD - 1:
                            ps_b[2].produced(s_t, tok)
                            ps_b[3].produced(s_t, tok)
                    T_(fst)

                def fvx(e, jh=jh, phase=phase):
                    ps_b[2].wait_ready(e)
                    ps_b[3].wait_ready(e)
                    if phase == 0 and jh == 0:
                        pool[12].acquire(e)
                    mu = pool[12].ap[0:1, :]
                    tok = None
                    for jj in (0, 1):
                        j = 2 * jh + jj
                        sl = slice(j * TCW, (j + 1) * TCW)
                        if phase == 0:
                            bi = e.tensor_scalar(mu[:, sl],
                                                 ps_b[2 + jj].ap[0:1, :],
                                                 1.0 / 1024.0, None, OP.mult)
                        else:
                            if jh == 0 and jj == 0:
                                pool[10].acquire(e)
                            bi = e.tensor_scalar(pool[10].ap[0:1, sl],
                                                 ps_b[2 + jj].ap[0:1, :],
                                                 1.0 / 1024.0, None, OP.mult)
                        tok = s_v.inc(bi)
                        ps_b[2 + jj].consumed(s_v, tok)
                    if phase == 0 and jh == 1:
                        pool[12].produced(s_v, tok)
                    if phase == 1:
                        hsl = slice(jh * 2 * TCW, (jh + 1) * 2 * TCW)
                        pool[12].wait_ready(e)
                        e.tensor_mul(pool[11].ap[0:1, hsl],
                                     pool[12].ap[0:1, hsl],
                                     pool[12].ap[0:1, hsl])
                        bi = e.tensor_sub(pool[10].ap[0:1, hsl],
                                          pool[10].ap[0:1, hsl],
                                          pool[11].ap[0:1, hsl])
                        tok = s_v.inc(bi)
                        pool[11].consumed(s_v, tok)
                        pool[12].consumed(s_v, tok)
                        pool[10].produced(s_v, tok)
                V_(fvx)
                if phase == 1:
                    def frs(e, jh=jh):
                        pool[10].wait_ready(e)
                        cst_sb.wait_ready(e)
                        hsl = slice(jh * 2 * TCW, (jh + 1) * 2 * TCW)
                        s1 = pool[10].ap[0:1, hsl]
                        e.activation(s1, s1, AF.Ln, bias=cst_sb.ap[0:1, 0:1])
                        if jh == 0:
                            srr.acquire(e)
                        bi = e.activation(srr.ap[:, hsl], s1, AF.Exp,
                                          scale=cst_sb.ap[0:1, 2:3])
                        tok = s_a.inc(bi)
                        pool[10].consumed(s_a, tok)
                        cst_sb.consumed(s_a, tok)
                        srr.produced(s_a, tok)
                    A_(frs)
            yield
        def fmus(e, b=b):
            pool[12].wait_ready(e)
            bi = e.dma_start(out=mu_d[b, 0:1, :],
                             in_=pool[12].ap[0:1, :].bitcast(F32R))
            s = g_st.nxt()
            tok = s.inc(bi, 16)
            pool[12].consumed(s, tok)
            store_toks[("mu", b)] = (s, tok)
            srr.wait_ready(e)
            bi = e.dma_start(out=mu_d[b, 1:2, :], in_=srr.ap)
            s = g_st.nxt()
            tok = s.inc(bi, 16)
            srr.consumed(s, tok)
            store_toks[("rr", b)] = (s, tok)
        G_(fmus)
        for half, key, dstb in ((0, "mu", pool[10]), (1, "rr", pool[12])):
            def fbl(e, half=half, key=key, dstb=dstb, b=b):
                dstb.acquire(e)
                ss, tt = store_toks[(key, b)]
                Buf._wait(e, ss, tt)
                bi = e.dma_start(out=dstb.ap.bitcast(F32R),
                                 in_=mu_d[b, half:half + 1, :]
                                 .broadcast_to([128, T]))
                s = g_st.nxt()
                dstb.produced(s, s.inc(bi, 16))
            G_(fbl)
        yield
        for k in range(KD):
            def fap(e, k=k):
                pool[10].wait_ready(e)
                pool[12].wait_ready(e)
                pool[11].acquire(e)
                e.tensor_sub(pool[11].ap, pool[k].ap, pool[10].ap)
                bi = e.tensor_mul(pool[k].ap, pool[11].ap, pool[12].ap)
                tok = s_v.inc(bi)
                pool[10].consumed(s_v, tok)
                pool[12].consumed(s_v, tok)
                pool[11].consumed(s_v, tok)
                pool[k].produced(s_v, tok)
            V_(fap)
            store(pool[k], xn_d[b, k])
            yield

    def ffn_stage(b, w_ap, in_d, inname, out_d, outname, is_last):
        pcnt = [0]
        for j in range(NTC):
            for kh in range(2):
                def fld(e, j=j, kh=kh):
                    ld_r[kh].acquire(e)
                    for q in range(4):
                        k = 4 * kh + q
                        rawkey = (inname, b, k, j)
                        if rawkey in store_toks:
                            ss, tt = store_toks[rawkey]
                            Buf._wait(e, ss, tt)
                        bi = e.dma_start(
                            out=ld_r[kh].ap[:, q * TCW:(q + 1) * TCW],
                            in_=in_d[b, k, :, j * TCW:(j + 1) * TCW])
                        s = g_st.nxt()
                        ld_r[kh].produced(s, s.inc(bi, 16))
                G_(fld)
            for m in range(KD):
                wslot = wt_f[m % 2]

                def fw(e, m=m, wslot=wslot):
                    wslot.acquire(e)
                    bi = e.dma_start(out=wslot.ap, in_=w_ap[m])
                    s = g_st.nxt()
                    wslot.produced(s, s.inc(bi, 16))
                G_(fw)

                bank = ps_b[pcnt[0] % 2]
                pcnt[0] += 1

                def ftk(e, m=m, wslot=wslot, bank=bank):
                    wslot.wait_ready(e)
                    ld_r[0].wait_ready(e)
                    ld_r[1].wait_ready(e)
                    bank.acquire(e)
                    tok = None
                    for k in range(KD):
                        bi = e.matmul(bank.ap,
                                      wslot.ap[:, k * 128:(k + 1) * 128],
                                      ld_r[k // 4].ap[:, (k % 4) * TCW:
                                                      (k % 4 + 1) * TCW],
                                      start=(k == 0), stop=(k == KD - 1))
                        tok = s_t.inc(bi)
                    wslot.consumed(s_t, tok)
                    ld_r[0].consumed(s_t, tok)
                    ld_r[1].consumed(s_t, tok)
                    bank.produced(s_t, tok)
                T_(ftk)

                slot = pool[8 + m % 2]

                def fe(e, m=m, slot=slot, bank=bank):
                    slot.acquire(e)
                    bank.wait_ready(e)
                    if is_last:
                        ffnb_sb.wait_ready(e)
                        bi = e.activation(slot.ap[:, 0:TCW], bank.ap, AF.Tanh,
                                          bias=ffnb_sb.ap[:, m:m + 1])
                        tok = s_a.inc(bi)
                        ffnb_sb.consumed(s_a, tok)
                    else:
                        bi = e.activation(slot.ap[:, 0:TCW], bank.ap, AF.Copy)
                        tok = s_a.inc(bi)
                    bank.consumed(s_a, tok)
                    slot.produced(s_a, tok)
                A_(fe)
                store(slot, out_d[b, m, :, j * TCW:(j + 1) * TCW],
                      src_ap=slot.ap[:, 0:TCW], key=(outname, b, m, j))
                yield

    def ffn(b):
        loads_barrier()
        yield from ffn_stage(b, ffnw, xn_d, "xn", h1_d, "h1", False)
        yield from ffn_stage(b, ffna1, h1_d, "h1", h2_d, "h2", False)
        yield from ffn_stage(b, ffna2, h2_d, "h2", ffn_d, "ffn", True)

    def load_layer_weights(l):
        loadg(xpwT_sb, W[f"xpwT{l}"])
        load(convw_sb, W[f"convw{l}"])
        load(convb_sb, W[f"convb{l}"])
        load(dtb_sb, W[f"dtb{l}"])
        load(dsk_sb, W[f"dsk{l}"])
        load(lnw_sb, W[f"lnw{l}"])
        load(lnb_sb, W[f"lnb{l}"])
        load(A_sb, W[f"alog{l}"])

        def fa(e):
            A_sb.wait_ready(e)
            bi = e.activation(A_sb.ap, A_sb.ap, AF.Exp)
            A_sb.produced(s_a, s_a.inc(bi))
        A_(fa)

        def fv(e):
            A_sb.wait_ready(e)
            bi = e.tensor_scalar(A_sb.ap, A_sb.ap, -1.0, None, OP.mult)
            A_sb.produced(s_v, s_v.inc(bi))
        V_(fv)

    def gen_A(l, b, stream_d):
        loads_barrier()
        bxc = Buf(pool[11].ap[:, 0:515])
        bcv = Buf(pool[11].ap[:, 1024:1536])
        buz = Buf(pool[11].ap[:, 1536:2048])
        pool[11].cons = {}
        jm = [0]
        for j in range(NTC):
            for kh in range(2):
                def fls(e, j=j, kh=kh):
                    pool[kh].acquire(e)
                    for q in range(4):
                        bi = e.dma_start(
                            out=pool[kh].ap[:, q * TCW:(q + 1) * TCW],
                            in_=stream_d[b, 4 * kh + q, :,
                                         j * TCW:(j + 1) * TCW])
                        s = g_ld.nxt()
                        pool[kh].produced(s, s.inc(bi, 16))
                L_(fls)
            for m in range(2 * KI):
                dtile = m % KI
                is_z = m >= KI
                wslot = wt_in[m % 2]

                def fw(e, mm_=None):
                    wslot_ = wt_in[mm_ % 2]
                    wslot_.acquire(e)
                    bi = e.dma_start(out=wslot_.ap, in_=W[f"inwT{l}"][mm_])
                    s = g_ld.nxt()
                    wslot_.produced(s, s.inc(bi, 16))
                if m == 0:
                    L_(lambda e: fw(e, mm_=0))
                if m + 1 < 2 * KI:
                    L_(lambda e, m=m: fw(e, mm_=m + 1))

                bank = ps_b[3]
                jm[0] += 1

                def ft(e, j=j, wslot=wslot, bank=bank):
                    wslot.wait_ready(e)
                    pool[0].wait_ready(e)
                    pool[1].wait_ready(e)
                    bank.acquire(e)
                    tok = None
                    for k in range(KD):
                        bi = e.matmul(bank.ap,
                                      wslot.ap[:, k * 128:(k + 1) * 128],
                                      pool[k // 4].ap[:, (k % 4) * TCW:
                                                      (k % 4 + 1) * TCW],
                                      start=(k == 0), stop=(k == KD - 1))
                        tok = s_t.inc(bi)
                    wslot.consumed(s_t, tok)
                    pool[0].consumed(s_t, tok)
                    pool[1].consumed(s_t, tok)
                    bank.produced(s_t, tok)
                T_(ft)

                if not is_z:
                    def fe(e, bank=bank, dtile=dtile):
                        bxc.acquire(e)
                        bank.wait_ready(e)
                        hist_sb.wait_ready(e)
                        bi = e.activation(pool[11].ap[:, 3:3 + TCW],
                                          bank.ap, AF.Copy)
                        tok = s_a.inc(bi)
                        bank.consumed(s_a, tok)
                        bxc.produced(s_a, tok)
                    A_(fe)

                    def fcv(e, dtile=dtile, j=j):
                        bxc.wait_ready(e)
                        convw_sb.wait_ready(e)
                        bcv.acquire(e)
                        if j == 0:
                            e.memset(pool[11].ap[:, 0:3], 0.0)
                        else:
                            hist_sb.wait_ready(e)
                            e.tensor_copy(pool[11].ap[:, 0:3],
                                          hist_sb.ap[:, dtile * 4:dtile * 4 + 3])
                        e.tensor_scalar(bcv.ap, pool[11].ap[:, 3:3 + TCW],
                                        convw_sb.ap[:, dtile * 4 + 3:dtile * 4 + 4],
                                        None, OP.mult)
                        for tap in (0, 1, 2):
                            sh = 3 - tap
                            bi = e.scalar_tensor_tensor(
                                bcv.ap, pool[11].ap[:, 3 - sh:3 - sh + TCW],
                                convw_sb.ap[:, dtile * 4 + tap:dtile * 4 + tap + 1],
                                bcv.ap, OP.mult, OP.add)
                        hist_sb.acquire(e)
                        bi2 = e.tensor_copy(hist_sb.ap[:, dtile * 4:dtile * 4 + 3],
                                            pool[11].ap[:, TCW:TCW + 3])
                        tok = s_v.inc(bi2)
                        bxc.consumed(s_v, tok)
                        convw_sb.consumed(s_v, tok)
                        bcv.produced(s_v, tok)
                        hist_sb.produced(s_v, tok)
                    V_(fcv)

                    def fu(e, dtile=dtile, j=j):
                        bcv.wait_ready(e)
                        convb_sb.wait_ready(e)
                        buz.acquire(e)
                        bi = e.activation(buz.ap, bcv.ap, AF.Silu,
                                          bias=convb_sb.ap[:, dtile:dtile + 1])
                        tok = s_a.inc(bi)
                        buz.produced(s_a, tok)
                        u_r.acquire(e)
                        bi = e.activation(u_r.ap[:, j * TCW:(j + 1) * TCW],
                                          bcv.ap, AF.Silu,
                                          bias=convb_sb.ap[:, dtile:dtile + 1])
                        tok = s_a.inc(bi)
                        bcv.consumed(s_a, tok)
                        convb_sb.consumed(s_a, tok)
                        u_r.produced(s_a, tok)
                    A_(fu)
                    store(buz, u_d[b, dtile, :, j * TCW:(j + 1) * TCW])

                    def fxp(e, dtile=dtile, j=j):
                        u_r.wait_ready(e)
                        xpwT_sb.wait_ready(e)
                        if dtile == 0 and j == 0:
                            ps_a.acquire(e)
                        bi = e.matmul(ps_a.ap[0:96, j * TCW:(j + 1) * TCW],
                                      xpwT_sb.ap[:, dtile * 96:(dtile + 1) * 96],
                                      u_r.ap[:, j * TCW:(j + 1) * TCW],
                                      start=(dtile == 0), stop=(dtile == KI - 1))
                        tok = s_t.inc(bi)
                        u_r.consumed(s_t, tok)
                        xpwT_sb.consumed(s_t, tok)
                        if dtile == KI - 1 and j == NTC - 1:
                            ps_a.produced(s_t, tok)
                    T_(fxp)
                else:
                    def fz(e, bank=bank, dtile=dtile, j=j):
                        buz.acquire(e)
                        bank.wait_ready(e)
                        bi = e.activation(buz.ap, bank.ap, AF.Silu)
                        tok = s_a.inc(bi)
                        bank.consumed(s_a, tok)
                        buz.produced(s_a, tok)
                    A_(fz)
                    store(buz, zs_d[b, dtile, :, j * TCW:(j + 1) * TCW])
                yield

        def fxe(e, b=b):
            ps_a.wait_ready(e)
            xdbl[b].acquire(e)
            bi = e.activation(xdbl[b].ap, ps_a.ap[0:96, :], AF.Copy)
            tok = s_a.inc(bi)
            ps_a.consumed(s_a, tok)
            xdbl[b].produced(s_a, tok)
        A_(fxe)

        def fxs(e, b=b):
            xdbl[b].wait_ready(e)
            bi = e.dma_start(out=xdbl_d[b], in_=xdbl[b].ap)
            s = g_st.nxt()
            tok = s.inc(bi, 16)
            xdbl[b].consumed(s, tok)
            store_toks[("xdbl", b)] = (s, tok)
        G_(fxs)
        yield

    def gen_scan(l, b):
        loads_barrier()
        for g in range(NGRP):
            for i in range(G):
                dtile = g * G + i
                loadg(dtw1, W[f"dtwT{l}"][:, dtile * 128:(dtile + 1) * 128])
                for j in range(NTC):
                    bank = ps_b[2]

                    def fdt(e, dtile=dtile, j=j, bank=bank, i=i):
                        dtw1.wait_ready(e)
                        xdbl[b].wait_ready(e)
                        bank.acquire(e)
                        bi = e.matmul(bank.ap, dtw1.ap,
                                      xdbl[b].ap[0:64, j * TCW:(j + 1) * TCW],
                                      start=True, stop=True)
                        tok = s_t.inc(bi)
                        xdbl[b].consumed(s_t, tok)
                        dtw1.consumed(s_t, tok)
                        bank.produced(s_t, tok)
                    T_(fdt)

                    def fspx(e, dtile=dtile, j=j, bank=bank, i=i):
                        if j == 0:
                            pool[4 + i].acquire(e)
                        bank.wait_ready(e)
                        dtb_sb.wait_ready(e)
                        bi = e.activation(pool[4 + i].ap[:, j * TCW:(j + 1) * TCW],
                                          bank.ap, AF.Exp,
                                          bias=dtb_sb.ap[:, dtile:dtile + 1])
                        tok = s_a.inc(bi)
                        bank.consumed(s_a, tok)
                        dtb_sb.consumed(s_a, tok)
                        if j == NTC - 1:
                            bi = e.activation(pool[4 + i].ap, pool[4 + i].ap,
                                              AF.Ln, bias=cst_sb.ap[:, 1:2])
                            tok = s_a.inc(bi)
                            cst_sb.consumed(s_a, tok)
                        pool[4 + i].produced(s_a, tok)
                    A_(fspx)

                uslot = pool[9] if i == 0 else pool[12]
                load(uslot, u_d[b, dtile])

                def fdtu(e, i=i, uslot=uslot):
                    pool[4 + i].wait_ready(e)
                    uslot.wait_ready(e)
                    pool[6 + i].acquire(e)
                    bi = e.tensor_mul(pool[6 + i].ap, pool[4 + i].ap, uslot.ap)
                    tok = s_v.inc(bi)
                    pool[4 + i].consumed(s_v, tok)
                    uslot.consumed(s_v, tok)
                    pool[6 + i].produced(s_v, tok)
                V_(fdtu)
                yield

            def em_bload(n, half):
                dstb = ld_r[half]
                row = (64 if half == 0 else 80) + n
                def fbl(e, dstb=dstb, row=row):
                    dstb.acquire(e)
                    ss, tt = store_toks[("xdbl", b)]
                    Buf._wait(e, ss, tt)
                    bi = e.dma_start(out=dstb.ap,
                                     in_=xdbl_d[b, row:row + 1, :]
                                     .broadcast_to([128, T]))
                    s = g_st.nxt()
                    dstb.produced(s, s.inc(bi, 16))
                G_(fbl)

            def em_da(n, i):
                dtile = g * G + i
                ci = dtile * NST + n
                def fda(e, i=i, ci=ci):
                    pool[8].acquire(e)
                    pool[4 + i].wait_ready(e)
                    A_sb.wait_ready(e)
                    bi = e.activation(pool[8].ap, pool[4 + i].ap, AF.Exp,
                                      scale=A_sb.ap[:, ci:ci + 1])
                    tok = s_a.inc(bi)
                    pool[4 + i].consumed(s_a, tok)
                    A_sb.consumed(s_a, tok)
                    pool[8].produced(s_a, tok)
                A_(fda)

            def em_dve(n, i):
                pslot = pool[9] if (n * G + i) % 2 == 0 else pool[12]
                def fscan(e, i=i, n=n, pslot=pslot):
                    pslot.acquire(e)
                    ld_r[0].wait_ready(e)
                    pool[6 + i].wait_ready(e)
                    Bb32 = ld_r[0].ap.bitcast(F32)
                    bi = e.tensor_mul(pslot.ap, pool[6 + i].ap, Bb32)
                    tok = s_v.inc(bi)
                    ld_r[0].consumed(s_v, tok)
                    pool[6 + i].consumed(s_v, tok)
                    pslot.produced(s_v, tok)
                    pool[8].wait_ready(e)
                    pool[10].acquire(e)
                    bi = e.tensor_tensor_scan(pool[10].ap, pool[8].ap,
                                              pslot.ap, 0.0, OP.mult, OP.add)
                    tok = s_v.inc(bi)
                    pool[8].consumed(s_v, tok)
                    pslot.consumed(s_v, tok)
                    pool[10].produced(s_v, tok)
                    ld_r[1].wait_ready(e)
                    Cb32 = ld_r[1].ap.bitcast(F32)
                    if n == 0:
                        pool[2 + i].acquire(e)
                        bi = e.tensor_mul(pool[2 + i].ap, pool[10].ap, Cb32)
                        tok = s_v.inc(bi)
                        pool[2 + i].produced(s_v, tok)
                    else:
                        bi = e.tensor_mul(pslot.ap, pool[10].ap, Cb32)
                        tok = s_v.inc(bi)
                        pslot.produced(s_v, tok)
                    ld_r[1].consumed(s_v, tok)
                    pool[10].consumed(s_v, tok)
                V_(fscan)
                if n > 0:
                    def fadd(e, i=i, pslot=pslot):
                        pslot.wait_ready(e)
                        pool[2 + i].wait_ready(e)
                        bi = e.tensor_add(pool[2 + i].ap, pool[2 + i].ap,
                                          pslot.ap)
                        tok = s_v.inc(bi)
                        pslot.consumed(s_v, tok)
                        pool[2 + i].produced(s_v, tok)
                    V_(fadd)

            em_bload(0, 0)
            em_bload(0, 1)
            em_da(0, 0)
            for n in range(NST):
                em_dve(n, 0)
                em_da(n, 1)
                em_dve(n, 1)
                if n + 1 < NST:
                    em_bload(n + 1, 0)
                    em_da(n + 1, 0)
                    em_bload(n + 1, 1)
                yield

            for i in range(G):
                dtile = g * G + i
                zslot = pool[9]
                uslot = pool[12]
                load(zslot, zs_d[b, dtile])
                load(uslot, u_d[b, dtile])

                def fg(e, i=i, dtile=dtile, zslot=zslot, uslot=uslot):
                    pool[8].acquire(e)
                    uslot.wait_ready(e)
                    pool[2 + i].wait_ready(e)
                    dsk_sb.wait_ready(e)
                    bi = e.scalar_tensor_tensor(
                        pool[8].ap, uslot.ap,
                        dsk_sb.ap[:, dtile:dtile + 1], pool[2 + i].ap,
                        OP.mult, OP.add)
                    tok = s_v.inc(bi)
                    uslot.consumed(s_v, tok)
                    pool[2 + i].consumed(s_v, tok)
                    dsk_sb.consumed(s_v, tok)
                    pool[8].produced(s_v, tok)
                    zslot.wait_ready(e)
                    pool[10].acquire(e)
                    bi = e.tensor_mul(pool[10].ap, pool[8].ap, zslot.ap)
                    tok = s_v.inc(bi)
                    pool[8].consumed(s_v, tok)
                    zslot.consumed(s_v, tok)
                    pool[10].produced(s_v, tok)
                V_(fg)
                store(pool[10], yp_d[b, dtile])
                yield

    def gen_out_mm(l, b):
        loads_barrier()
        for m in range(KD):
            wslot = wt_out[m % 2]

            def fw(e, m=m, wslot=wslot):
                wslot.acquire(e)
                bi = e.dma_start(out=wslot.ap, in_=W[f"outwT{l}"][m])
                s = g_st.nxt()
                wslot.produced(s, s.inc(bi, 16))
            G_(fw)

            for k in range(KI):
                loadg(u_r, yp_d[b, k])

                def ftk(e, k=k, wslot=wslot):
                    wslot.wait_ready(e)
                    u_r.wait_ready(e)
                    if k == 0:
                        ps_a.acquire(e)
                    tok = None
                    for j in range(NTC):
                        bi = e.matmul(ps_a.ap[:, j * TCW:(j + 1) * TCW],
                                      wslot.ap[:, k * 128:(k + 1) * 128],
                                      u_r.ap[:, j * TCW:(j + 1) * TCW],
                                      start=(k == 0), stop=(k == KI - 1))
                        tok = s_t.inc(bi)
                    u_r.consumed(s_t, tok)
                    wslot.consumed(s_t, tok)
                    if k == KI - 1:
                        ps_a.produced(s_t, tok)
                T_(ftk)
                yield

            def fe(e, m=m):
                pool[11].acquire(e)
                ps_a.wait_ready(e)
                bi = e.activation(pool[11].ap, ps_a.ap, AF.Copy)
                tok = s_a.inc(bi)
                ps_a.consumed(s_a, tok)
                pool[11].produced(s_a, tok)
            A_(fe)
            store(pool[11], h1_d[b, m], key=("o", b, m))
            yield

    def gen_ln(l, b, stream_d, next_d):
        otiles = [pool[2 + kk] for kk in range(KD)]
        for k in range(KD):
            def flo(e, k=k):
                otiles[k].acquire(e)
                if ("o", b, k) in store_toks:
                    ss, tt = store_toks[("o", b, k)]
                    Buf._wait(e, ss, tt)
                bi = e.dma_start(out=otiles[k].ap, in_=h1_d[b, k])
                s = g_ld.nxt()
                otiles[k].produced(s, s.inc(bi, 16))
            L_(flo)
        smu_ap = ld_r[0].ap[0:1, :]
        def fvinit(e):
            ld_r[0].acquire(e)
            pool[12].acquire(e)
        V_(fvinit)
        for phase in (0, 1):
            for jh in (0, 1):
                for k in range(KD):
                    if phase == 1:
                        def fsq(e, k=k):
                            pool[10].acquire(e)
                            otiles[k].wait_ready(e)
                            bi = e.activation(pool[10].ap, otiles[k].ap,
                                              AF.Square)
                            tok = s_a.inc(bi)
                            pool[10].produced(s_a, tok)
                            otiles[k].consumed(s_a, tok)
                        A_(fsq)

                    def fst(e, k=k, jh=jh, phase=phase):
                        srcb = otiles[k] if phase == 0 else pool[10]
                        srcb.wait_ready(e)
                        tok = None
                        for jj in (0, 1):
                            j = 2 * jh + jj
                            if k == 0:
                                ps_b[jj].acquire(e)
                            bi = e.matmul(ps_b[jj].ap[0:1, :], onesc_sb.ap,
                                          srcb.ap[:, j * TCW:(j + 1) * TCW],
                                          start=(k == 0), stop=(k == KD - 1))
                            tok = s_t.inc(bi)
                        srcb.consumed(s_t, tok)
                        if k == KD - 1:
                            ps_b[0].produced(s_t, tok)
                            ps_b[1].produced(s_t, tok)
                    T_(fst)

                def fvx(e, jh=jh, phase=phase):
                    ps_b[0].wait_ready(e)
                    ps_b[1].wait_ready(e)
                    mu2 = pool[12].ap[0:1, :]
                    tok = None
                    for jj in (0, 1):
                        j = 2 * jh + jj
                        sl = slice(j * TCW, (j + 1) * TCW)
                        if phase == 0:
                            e.tensor_scalar(smu_ap[:, sl], ps_b[jj].ap[0:1, :],
                                            1.0 / 1024.0, None, OP.mult)
                            bi = e.tensor_scalar(mu2[:, sl],
                                                 ps_b[jj].ap[0:1, :],
                                                 1.0 / 1024.0, None, OP.mult)
                        else:
                            if jj == 0:
                                pool[10].acquire(e)
                            bi = e.tensor_scalar(pool[10].ap[0:1, sl],
                                                 ps_b[jj].ap[0:1, :],
                                                 1.0 / 1024.0, None, OP.mult)
                        tok = s_v.inc(bi)
                        ps_b[jj].consumed(s_v, tok)
                    if phase == 0 and jh == 1:
                        bi = e.tensor_mul(smu_ap, mu2, mu2)
                        tok = s_v.inc(bi)
                        ld_r[0].produced(s_v, tok)
                        pool[12].produced(s_v, tok)
                    if phase == 1:
                        hsl = slice(jh * 2 * TCW, (jh + 1) * 2 * TCW)
                        ld_r[0].wait_ready(e)
                        bi = e.tensor_sub(pool[10].ap[0:1, hsl],
                                          pool[10].ap[0:1, hsl],
                                          smu_ap[:, hsl].bitcast(F32))
                        tok = s_v.inc(bi)
                        ld_r[0].consumed(s_v, tok)
                        pool[10].produced(s_v, tok)
                V_(fvx)
                if phase == 1:
                    def frs(e, jh=jh):
                        pool[10].wait_ready(e)
                        cst_sb.wait_ready(e)
                        hsl = slice(jh * 2 * TCW, (jh + 1) * 2 * TCW)
                        s1 = pool[10].ap[0:1, hsl]
                        e.activation(s1, s1, AF.Ln, bias=cst_sb.ap[0:1, 0:1])
                        if jh == 0:
                            srr.acquire(e)
                        bi = e.activation(srr.ap[:, hsl], s1, AF.Exp,
                                          scale=cst_sb.ap[0:1, 2:3])
                        tok = s_a.inc(bi)
                        pool[10].consumed(s_a, tok)
                        cst_sb.consumed(s_a, tok)
                        srr.produced(s_a, tok)
                    A_(frs)
            yield
        def fmus(e):
            pool[12].wait_ready(e)
            bi = e.dma_start(out=mu_d[b, 0:1, :],
                             in_=pool[12].ap[0:1, :].bitcast(F32R))
            s = g_st.nxt()
            tok = s.inc(bi, 16)
            pool[12].consumed(s, tok)
            store_toks[("mu", b)] = (s, tok)
            srr.wait_ready(e)
            bi = e.dma_start(out=mu_d[b, 1:2, :], in_=srr.ap)
            s = g_st.nxt()
            tok = s.inc(bi, 16)
            srr.consumed(s, tok)
            store_toks[("rr", b)] = (s, tok)
        G_(fmus)

        for half, key in ((0, "mu"), (1, "rr")):
            def fbl(e, half=half, key=key):
                ld_r[half].acquire(e)
                ss, tt = store_toks[(key, b)]
                Buf._wait(e, ss, tt)
                bi = e.dma_start(out=ld_r[half].ap,
                                 in_=mu_d[b, half:half + 1, :]
                                 .broadcast_to([128, T]))
                s = g_st.nxt()
                ld_r[half].produced(s, s.inc(bi, 16))
            G_(fbl)
        yield

        for k in range(KD):
            def fap(e, k=k):
                ld_r[0].wait_ready(e)
                ld_r[1].wait_ready(e)
                if k == 0:
                    pool[10].acquire(e)
                lnw_sb.wait_ready(e)
                lnb_sb.wait_ready(e)
                mub = ld_r[0].ap.bitcast(F32)
                rrb = ld_r[1].ap.bitcast(F32)
                e.tensor_sub(pool[10].ap, otiles[k].ap, mub)
                e.tensor_mul(pool[10].ap, pool[10].ap, rrb)
                bi = e.tensor_scalar(otiles[k].ap, pool[10].ap,
                                     lnw_sb.ap[:, k:k + 1],
                                     lnb_sb.ap[:, k:k + 1],
                                     OP.mult, OP.add)
                tok = s_v.inc(bi)
                ld_r[0].consumed(s_v, tok)
                ld_r[1].consumed(s_v, tok)
                lnw_sb.consumed(s_v, tok)
                lnb_sb.consumed(s_v, tok)
                otiles[k].produced(s_v, tok)
            V_(fap)
        yield
        for k in range(KD):
            load(pool[12], stream_d[b, k])

            def fres(e, k=k):
                pool[12].wait_ready(e)
                otiles[k].wait_ready(e)
                pool[10].acquire(e)
                bi = e.tensor_add(pool[10].ap, otiles[k].ap, pool[12].ap)
                tok = s_v.inc(bi)
                pool[12].consumed(s_v, tok)
                otiles[k].consumed(s_v, tok)
                pool[10].produced(s_v, tok)
            V_(fres)
            store(pool[10], next_d[b, k])
            yield

    def run_gen(gen):
        for _ in gen:
            pass

    def interleave(g1, g2):
        done1 = done2 = False
        while not (done1 and done2):
            if not done1:
                try:
                    next(g1)
                except StopIteration:
                    done1 = True
            if not done2:
                try:
                    next(g2)
                except StopIteration:
                    done2 = True

    def final(b):
        loads_barrier()
        for k in range(KD):
            base = (k % 2) * 4
            i0, i1, i2, i3 = base, base + 1, base + 2, base + 3
            load(pool[i0], xfm[b, k])
            load(pool[i1], m2_d[b, k])
            load(pool[i2], ffn_d[b, k])

            def fv(e, i0=i0, i1=i1, i2=i2, i3=i3):
                pool[i0].wait_ready(e)
                pool[i1].wait_ready(e)
                pool[i2].wait_ready(e)
                pool[i3].acquire(e)
                e.tensor_add(pool[i3].ap, pool[i0].ap, pool[i1].ap)
                bi = e.tensor_add(pool[i3].ap, pool[i3].ap, pool[i2].ap)
                tok = s_v.inc(bi)
                pool[i0].consumed(s_v, tok)
                pool[i1].consumed(s_v, tok)
                pool[i2].consumed(s_v, tok)
                pool[i3].produced(s_v, tok)
            V_(fv)
            store(pool[i3], outfm[b, k])

    run_gen(p0_ln(0))
    flush()
    interleave(p0_ln(1), ffn(0))
    flush()
    ffn1_pending = ffn(1)
    _NL = int(_os.environ.get("KD_NLAYERS", "2"))
    pending_ln = None
    for l in range(_NL):
        stream_d = xn_d if l == 0 else m1_d
        next_d = m1_d if l == 0 else m2_d
        load_layer_weights(l)
        flush()
        if pending_ln is not None:
            interleave(pending_ln, gen_A(l, 0, stream_d))
            pending_ln = None
        elif ffn1_pending is not None:
            interleave(ffn1_pending, gen_A(l, 0, stream_d))
            ffn1_pending = None
        else:
            run_gen(gen_A(l, 0, stream_d))
        flush()
        interleave(gen_scan(l, 0), gen_A(l, 1, stream_d))
        flush()
        interleave(gen_scan(l, 1), gen_out_mm(l, 0))
        flush()
        interleave(gen_ln(l, 0, stream_d, next_d), gen_out_mm(l, 1))
        flush()
        pending_ln = gen_ln(l, 1, stream_d, next_d)
    if pending_ln is not None:
        run_gen(pending_ln)
        flush()
    for b in range(BC):
        final(b)
    flush()
    def f_fin_g(e):
        for s in g_st.sems:
            e.wait_ge(s.h, s.n)
    G_(f_fin_g)
    def f_fin_l(e):
        for s in g_ld.sems:
            e.wait_ge(s.h, s.n)
    L_(f_fin_l)
    def f_fin_a(e):
        for s in g_sc.sems:
            e.wait_ge(s.h, s.n)
    A_(f_fin_a)

    flush()

    st.close()
    nc._kd_trace = trace_log
    return nc


def _ffn_tile(w):
    w = np.asarray(w).reshape(8, 128, 8, 128)
    return np.ascontiguousarray(w.transpose(2, 1, 0, 3)).reshape(8, 128, 1024)


def host_inputs(inputs, core):
    asc = np.ascontiguousarray
    x = np.asarray(inputs["L0_attn_x"])[2 * core:2 * core + 2]
    d = {
        "xfm": asc(x.transpose(0, 2, 1)).reshape(BC, KD, 128, T),
        "ffnw": _ffn_tile(inputs["L0_ffn_weight"]),
        "ffna1": _ffn_tile(inputs["L0_ffn_aux_weight1"]),
        "ffna2": _ffn_tile(inputs["L0_ffn_aux_weight2"]),
        "ffnb": asc(np.asarray(inputs["L0_ffn_bias"]).reshape(8, 128).T),
        "onesr": np.ones((1, 128), np.float32),
        "onesc": np.ones((128, 1), np.float32),
        "cst": np.repeat(np.array([[EPS, 1.0, -0.5, 0.0]], np.float32), 128, axis=0),
    }
    for l in (0, 1):
        inwT = np.asarray(inputs["L0_attn_in_proj_w"])[l].T
        d[f"inwT{l}"] = asc(inwT.reshape(8, 128, 32, 128)
                            .transpose(2, 1, 0, 3)).reshape(32, 128, 8 * 128)
        outwT = np.asarray(inputs["L0_attn_out_proj_w"])[l].T
        d[f"outwT{l}"] = asc(outwT.reshape(16, 128, 8, 128)
                             .transpose(2, 1, 0, 3)).reshape(8, 128, 16 * 128)
        d[f"dtwT{l}"] = asc(np.asarray(inputs["L0_attn_dt_proj_w"])[l].T)
        d[f"xpwT{l}"] = asc(np.asarray(inputs["L0_attn_x_proj_w"])[l].T
                            .reshape(16, 128, 96).transpose(1, 0, 2)).reshape(128, 16 * 96)
        d[f"convw{l}"] = asc(np.asarray(inputs["L0_attn_conv_w"])[l]
                             .reshape(16, 128, 4).transpose(1, 0, 2)).reshape(128, 64)
        d[f"convb{l}"] = asc(np.asarray(inputs["L0_attn_conv_b"])[l].reshape(16, 128).T)
        d[f"dtb{l}"] = asc(np.asarray(inputs["L0_attn_dt_proj_b"])[l].reshape(16, 128).T)
        d[f"dsk{l}"] = asc(np.asarray(inputs["L0_attn_Dskip"])[l].reshape(16, 128).T)
        d[f"alog{l}"] = asc(np.asarray(inputs["L0_attn_A_log"])[l]
                            .reshape(16, 128, 16).transpose(1, 0, 2)).reshape(128, 256)
        d[f"lnw{l}"] = asc(np.asarray(inputs["L0_attn_ln_w"])[l].reshape(8, 128).T)
        d[f"lnb{l}"] = asc(np.asarray(inputs["L0_attn_ln_b"])[l].reshape(8, 128).T)
    return {k: np.ascontiguousarray(v, dtype=np.float32) for k, v in d.items()}


def kernel(**inputs):
    nc = build_nc()
    in_maps = [host_inputs(inputs, c) for c in range(NCORE)]
    res = run_bass_kernel_spmd(nc, in_maps, core_ids=list(range(NCORE)))
    outs = []
    for c in range(NCORE):
        o = np.asarray(res.results[c]["outfm"])
        o = o.reshape(BC, 1024, T).transpose(0, 2, 1)
        outs.append(o)
    return np.ascontiguousarray(np.concatenate(outs, axis=0), dtype=np.float32)


# version 12
# speedup vs baseline: 1.0898x; 1.0898x over previous
import numpy as np
from contextlib import ExitStack
import concourse.bass as bass
import concourse.mybir as mybir
from concourse.bass_utils import run_bass_kernel_spmd

F32 = mybir.dt.float32
F32R = mybir.dt.float32r
AF = mybir.ActivationFunctionType
OP = mybir.AluOpType

NCORE = 8
BC = 2
T = 2048
KD = 8
KI = 16
NST = 16
TCW = 512
NTC = T // TCW
G = 2
NGRP = KI // G
EPS = 1e-5


class Sem:
    def __init__(self, st, nc, name):
        self.h = st.enter_context(nc.semaphore(name))
        self.n = 0

    def inc(self, bi, amt=1):
        bi.then_inc(self.h, amt)
        self.n += amt
        return self.n


class SemGroup:
    def __init__(self, st, nc, name, k):
        self.sems = [Sem(st, nc, f"{name}{i}") for i in range(k)]
        self.i = 0

    def nxt(self):
        s = self.sems[self.i]
        self.i = (self.i + 1) % len(self.sems)
        return s


class Buf:
    CUR = ['?']
    WAITED = {}

    @classmethod
    def _wait(cls, eng, s, t):
        key = (cls.CUR[0], id(s))
        if cls.WAITED.get(key, -1) >= t:
            return
        eng.wait_ge(s.h, t)
        cls.WAITED[key] = t

    def __init__(self, ap):
        self.ap = ap
        self.ready = {}
        self.cons = {}

    def acquire(self, eng):
        for s, t in self.cons.items():
            Buf._wait(eng, s, t)
        self.cons = {}
        self.ready = {}

    def produced(self, s, tok):
        if s in self.ready:
            self.ready[s] = max(self.ready[s], tok)
        else:
            self.ready[s] = tok

    def wait_ready(self, eng):
        for s, t in self.ready.items():
            Buf._wait(eng, s, t)

    def consumed(self, s, tok):
        if s in self.cons:
            self.cons[s] = max(self.cons[s], tok)
        else:
            self.cons[s] = tok


def build_nc():
    nc = bass.Bass("TRN2", target_bir_lowering=False)

    def din(name, shape):
        return nc.dram_tensor(name, shape, F32, kind="ExternalInput").ap()

    def dout(name, shape):
        return nc.dram_tensor(name, shape, F32, kind="ExternalOutput").ap()

    import os as _os
    _DBG = _os.environ.get("KD_DBG", "0") == "1"

    def dint(name, shape):
        kind = "ExternalOutput" if _DBG else "Internal"
        return nc.dram_tensor(name, shape, F32, kind=kind).ap()

    xfm = din("xfm", [BC, KD, 128, T])
    W = {}
    for l in (0, 1):
        W[f"inwT{l}"] = din(f"inwT{l}", [32, 128, 8 * 128])
        W[f"outwT{l}"] = nc.dram_tensor(f"outwT{l}", [8, 128, 16 * 128], F32R, kind="ExternalInput").ap()
        W[f"dtwT{l}"] = nc.dram_tensor(f"dtwT{l}", [64, 2048], F32R, kind="ExternalInput").ap()
        W[f"xpwT{l}"] = nc.dram_tensor(f"xpwT{l}", [128, 16 * 96], F32R, kind="ExternalInput").ap()
        W[f"convw{l}"] = din(f"convw{l}", [128, 64])
        W[f"convb{l}"] = din(f"convb{l}", [128, 16])
        W[f"dtb{l}"] = din(f"dtb{l}", [128, 16])
        W[f"dsk{l}"] = din(f"dsk{l}", [128, 16])
        W[f"alog{l}"] = din(f"alog{l}", [128, 256])
        W[f"lnw{l}"] = din(f"lnw{l}", [128, 8])
        W[f"lnb{l}"] = din(f"lnb{l}", [128, 8])
    ffnw = nc.dram_tensor("ffnw", [8, 128, 1024], F32R, kind="ExternalInput").ap()
    ffna1 = nc.dram_tensor("ffna1", [8, 128, 1024], F32R, kind="ExternalInput").ap()
    ffna2 = nc.dram_tensor("ffna2", [8, 128, 1024], F32R, kind="ExternalInput").ap()
    ffnb = din("ffnb", [128, 8])
    onesr = din("onesr", [1, 128])
    onesc = din("onesc", [128, 1])
    cst = din("cst", [128, 4])

    outfm = dout("outfm", [BC, KD, 128, T])

    xn_d = dint("xn_d", [BC, KD, 128, T])
    ffn_d = dint("ffn_d", [BC, KD, 128, T])
    h1_d = nc.dram_tensor("h1_d", [BC, KD, 128, T], F32R,
                          kind="ExternalOutput" if _DBG else "Internal").ap()
    h2_d = nc.dram_tensor("h2_d", [BC, KD, 128, T], F32R,
                          kind="ExternalOutput" if _DBG else "Internal").ap()
    m1_d = dint("m1_d", [BC, KD, 128, T])
    m2_d = dint("m2_d", [BC, KD, 128, T])
    u_d = dint("u_d", [BC, KI, 128, T])
    xdbl_d = nc.dram_tensor("xdbl_d", [BC, 96, T], F32R,
                            kind="ExternalOutput" if _DBG else "Internal").ap()
    mu_d = nc.dram_tensor("mu_d", [BC, 2, T], F32R,
                          kind="ExternalOutput" if _DBG else "Internal").ap()
    zs_d = dint("zs_d", [BC, KI, 128, T])
    yp_d = nc.dram_tensor("yp_d", [BC, KI, 128, T], F32R,
                          kind="ExternalOutput" if _DBG else "Internal").ap()

    st = ExitStack()
    st.__enter__()

    def mksb(name, shape, dt=F32):
        return Buf(st.enter_context(nc.sbuf_tensor(name, shape, dt)).ap())

    NPOOL = 13
    pool = [mksb(f"pl{i}", [128, T]) for i in range(NPOOL)]
    conv_sb = pool[11]
    xdbl = [mksb(f"xdbl{b}", [96, T], F32R) for b in range(BC)]
    wt_in = [mksb("wt_in0", [128, 1024]), mksb("wt_in1", [128, 1024])]
    wt_f = [mksb("wt_f0", [128, 1024], F32R), mksb("wt_f1", [128, 1024], F32R)]
    ld_r = [mksb("ld_r0", [128, T], F32R), mksb("ld_r1", [128, T], F32R)]
    wt_out = [mksb("wt_out0", [128, 2048], F32R), mksb("wt_out1", [128, 2048], F32R)]
    xpwT_sb = mksb("xpwT_sb", [128, 1536], F32R)
    dtw1 = mksb("dtw1", [64, 128], F32R)
    hist_sb = mksb("hist_sb", [128, 64])
    u_r = mksb("u_r", [128, T], F32R)
    onesr_r = mksb("onesr_r", [1, 128], F32R)
    A_sb = mksb("A_sb", [128, 256])
    convw_sb = mksb("convw_sb", [128, 64])
    convb_sb = mksb("convb_sb", [128, 16])
    dtb_sb = mksb("dtb_sb", [128, 16])
    dsk_sb = mksb("dsk_sb", [128, 16])
    lnw_sb = mksb("lnw_sb", [128, 8])
    lnb_sb = mksb("lnb_sb", [128, 8])
    ffnb_sb = mksb("ffnb_sb", [128, 8])
    onesc_sb = mksb("onesc_sb", [128, 1])
    cst_sb = mksb("cst_sb", [128, 4])
    stgB = mksb("stgB", [1, T], F32R)
    stgC = mksb("stgC", [1, T], F32R)
    srr = stgC

    ps_a = Buf(st.enter_context(nc.psum_tensor("ps_a", [128, T], F32)).ap())
    ps_b = [Buf(st.enter_context(nc.psum_tensor(f"ps_b{j}", [128, TCW], F32)).ap())
            for j in range(NTC)]

    g_ld = SemGroup(st, nc, "s_ld", 4)
    g_st = SemGroup(st, nc, "s_st", 4)
    g_sc = SemGroup(st, nc, "s_sc", 2)
    s_g = Sem(st, nc, "s_g")
    s_t = Sem(st, nc, "s_t")
    s_v = Sem(st, nc, "s_v")
    s_a = Sem(st, nc, "s_a")

    block = st.enter_context(nc.Block())

    Buf.WAITED = {}

    class RecInstr:
        __slots__ = ("incs",)
        def __init__(self):
            self.incs = []
        def then_inc(self, sem, amt):
            self.incs.append((sem, amt))
            return self

    class RecEng:
        __slots__ = ("lst", "kind")
        def __init__(self, lst, kind):
            self.lst = lst
            self.kind = kind
        def __getattr__(self, name):
            lst = self.lst
            kind = self.kind
            def call(*a, **kw):
                ri = RecInstr()
                lst.append((name, a, kw, ri))
                trace_log.append((kind, name, a, kw, ri))
                return ri
            return call

    rec = {k: [] for k in 'tvalg'}
    trace_log = []

    _eng_of = {}

    def flush():
        _eng_of.setdefault('t', block.tensor)
        for k, emit in (('t', block.tensor), ('v', block.vector),
                        ('a', block.scalar), ('l', block.sync),
                        ('g', block.gpsimd)):
            lst = rec[k]
            if not lst:
                continue
            rec[k] = []
            def f(e, lst=lst):
                for name, a, kw, ri in lst:
                    bi = getattr(e, name)(*a, **kw)
                    for sem, amt in ri.incs:
                        bi.then_inc(sem, amt)
            emit(f)

    def T_(fn):
        Buf.CUR[0] = 't'
        fn(RecEng(rec['t'], 't'))

    def V_(fn):
        Buf.CUR[0] = 'v'
        fn(RecEng(rec['v'], 'v'))

    def A_(fn):
        Buf.CUR[0] = 'a'
        fn(RecEng(rec['a'], 'a'))

    def L_(fn):
        Buf.CUR[0] = 'l'
        fn(RecEng(rec['l'], 'l'))

    def G_(fn):
        Buf.CUR[0] = 'g'
        fn(RecEng(rec['g'], 'g'))

    import os
    _PIN = os.environ.get("KD_PIN", "0") == "1"
    _GADD = os.environ.get("KD_GADD", "1") == "1"

    def pin_act(idx):
        if not _PIN:
            return
        def f(e):
            e.add_instruction(mybir.InstLoadActFuncSet(
                name=nc.get_next_instruction_name(),
                act_func_set_id=idx, ins=[], outs=[]))
        A_(f)

    def load(buf, src_ap, dst_ap=None):
        def f(e):
            buf.acquire(e)
            bi = e.dma_start(out=dst_ap if dst_ap is not None else buf.ap,
                             in_=src_ap)
            s = g_ld.nxt()
            buf.produced(s, s.inc(bi, 16))
        L_(f)

    def loadg(buf, src_ap, dst_ap=None):
        def f(e):
            buf.acquire(e)
            bi = e.dma_start(out=dst_ap if dst_ap is not None else buf.ap,
                             in_=src_ap)
            s = g_st.nxt()
            buf.produced(s, s.inc(bi, 16))
        G_(f)

    def store(buf, dst_ap, src_ap=None, key=None):
        def f(e):
            buf.wait_ready(e)
            bi = e.dma_start(out=dst_ap,
                             in_=src_ap if src_ap is not None else buf.ap)
            s = g_st.nxt()
            tok = s.inc(bi, 16)
            buf.consumed(s, tok)
            if key is not None:
                store_toks[key] = (s, tok)
        G_(f)

    store_toks = {}

    def loads_barrier():
        tgts = [(s, s.n) for s in g_st.sems]
        def fl(e):
            for s, t in tgts:
                Buf._wait(e, s, t)
        L_(fl)
        def fg(e):
            for s, t in tgts:
                Buf._wait(e, s, t)
        G_(fg)

    load(onesc_sb, onesc)
    load(ffnb_sb, ffnb)
    load(cst_sb, cst)
    loadg(onesr_r, onesr)

    def ln_stats_and_bcast(tiles, sq_slot, st_slot):
        st1 = st_slot
        for k in range(KD):
            def fa(e, k=k):
                sq_slot.acquire(e)
                tiles[k].wait_ready(e)
                bi = e.activation(sq_slot.ap, tiles[k].ap, AF.Square)
                tok = s_a.inc(bi)
                sq_slot.produced(s_a, tok)
                tiles[k].consumed(s_a, tok)
            A_(fa)

            def ft(e, k=k):
                tiles[k].wait_ready(e)
                sq_slot.wait_ready(e)
                if k == 0:
                    ps_a.acquire(e)
                    for j in range(NTC):
                        ps_b[j].acquire(e)
                tok = None
                for j in range(NTC):
                    e.matmul(ps_a.ap[0:1, j * TCW:(j + 1) * TCW],
                             onesc_sb.ap, tiles[k].ap[:, j * TCW:(j + 1) * TCW],
                             start=(k == 0), stop=(k == KD - 1))
                    bi2 = e.matmul(ps_b[j].ap[0:1, :],
                                   onesc_sb.ap, sq_slot.ap[:, j * TCW:(j + 1) * TCW],
                                   start=(k == 0), stop=(k == KD - 1))
                    tok = s_t.inc(bi2)
                sq_slot.consumed(s_t, tok)
                tiles[k].consumed(s_t, tok)
                if k == KD - 1:
                    ps_a.produced(s_t, tok)
                    for j in range(NTC):
                        ps_b[j].produced(s_t, tok)
            T_(ft)

        def fv(e):
            ps_a.wait_ready(e)
            for j in range(NTC):
                ps_b[j].wait_ready(e)
            ld_r[0].acquire(e)
            smu_ap = ld_r[0].ap[0:1, :]
            st1.acquire(e)
            pool[12].acquire(e)
            mu2 = pool[12].ap[0:1, :]
            s1 = st1.ap[0:1, :]
            e.tensor_scalar(smu_ap, ps_a.ap[0:1, :], 1.0 / 1024.0, None, OP.mult)
            e.tensor_scalar(mu2, ps_a.ap[0:1, :], 1.0 / 1024.0, None, OP.mult)
            for j in range(NTC):
                e.tensor_scalar(s1[:, j * TCW:(j + 1) * TCW],
                                ps_b[j].ap[0:1, :], 1.0 / 1024.0, None, OP.mult)
            e.tensor_mul(mu2, mu2, mu2)
            bi = e.tensor_sub(s1, s1, mu2)
            tok = s_v.inc(bi)
            ps_a.consumed(s_v, tok)
            for j in range(NTC):
                ps_b[j].consumed(s_v, tok)
            pool[12].consumed(s_v, tok)
            ld_r[0].produced(s_v, tok)
            st1.produced(s_v, tok)
        V_(fv)

        def fa2(e):
            st1.wait_ready(e)
            cst_sb.wait_ready(e)
            s1 = st1.ap[0:1, :]
            e.activation(s1, s1, AF.Ln, bias=cst_sb.ap[0:1, 0:1])
            srr.acquire(e)
            bi = e.activation(srr.ap, s1, AF.Exp,
                              scale=cst_sb.ap[0:1, 2:3])
            tok = s_a.inc(bi)
            st1.consumed(s_a, tok)
            srr.produced(s_a, tok)
        A_(fa2)

        def ft2(e):
            ps_a.acquire(e)
            ld_r[0].wait_ready(e)
            smu_ap = ld_r[0].ap[0:1, :]
            srr.wait_ready(e)
            onesr_r.wait_ready(e)
            tok = None
            for j in range(NTC):
                ps_b[j].acquire(e)
                e.matmul(ps_a.ap[:, j * TCW:(j + 1) * TCW], onesr_r.ap,
                         smu_ap[:, j * TCW:(j + 1) * TCW], start=True, stop=True)
                bi2 = e.matmul(ps_b[j].ap, onesr_r.ap,
                               srr.ap[:, j * TCW:(j + 1) * TCW], start=True, stop=True)
                tok = s_t.inc(bi2)
            ld_r[0].consumed(s_t, tok)
            srr.consumed(s_t, tok)
            ps_a.produced(s_t, tok)
            for j in range(NTC):
                ps_b[j].produced(s_t, tok)
        T_(ft2)

    def p0_ln(b):
        for k in range(KD):
            load(pool[k], xfm[b, k])
        yield
        for phase in (0, 1):
            for jh in (0, 1):
                for k in range(KD):
                    if phase == 1:
                        def fsq(e, k=k):
                            pool[11].acquire(e)
                            pool[k].wait_ready(e)
                            bi = e.activation(pool[11].ap, pool[k].ap,
                                              AF.Square)
                            tok = s_a.inc(bi)
                            pool[11].produced(s_a, tok)
                            pool[k].consumed(s_a, tok)
                        A_(fsq)

                    def fst(e, k=k, jh=jh, phase=phase):
                        srcb = pool[k] if phase == 0 else pool[11]
                        srcb.wait_ready(e)
                        tok = None
                        for jj in (0, 1):
                            j = 2 * jh + jj
                            bank = ps_b[2 + jj]
                            if k == 0:
                                bank.acquire(e)
                            bi = e.matmul(bank.ap[0:1, :], onesc_sb.ap,
                                          srcb.ap[:, j * TCW:(j + 1) * TCW],
                                          start=(k == 0), stop=(k == KD - 1))
                            tok = s_t.inc(bi)
                        srcb.consumed(s_t, tok)
                        if k == KD - 1:
                            ps_b[2].produced(s_t, tok)
                            ps_b[3].produced(s_t, tok)
                    T_(fst)

                def fvx(e, jh=jh, phase=phase):
                    ps_b[2].wait_ready(e)
                    ps_b[3].wait_ready(e)
                    if phase == 0 and jh == 0:
                        pool[12].acquire(e)
                    mu = pool[12].ap[0:1, :]
                    tok = None
                    for jj in (0, 1):
                        j = 2 * jh + jj
                        sl = slice(j * TCW, (j + 1) * TCW)
                        if phase == 0:
                            bi = e.tensor_scalar(mu[:, sl],
                                                 ps_b[2 + jj].ap[0:1, :],
                                                 1.0 / 1024.0, None, OP.mult)
                        else:
                            if jh == 0 and jj == 0:
                                pool[10].acquire(e)
                            bi = e.tensor_scalar(pool[10].ap[0:1, sl],
                                                 ps_b[2 + jj].ap[0:1, :],
                                                 1.0 / 1024.0, None, OP.mult)
                        tok = s_v.inc(bi)
                        ps_b[2 + jj].consumed(s_v, tok)
                    if phase == 0 and jh == 1:
                        pool[12].produced(s_v, tok)
                    if phase == 1:
                        hsl = slice(jh * 2 * TCW, (jh + 1) * 2 * TCW)
                        pool[12].wait_ready(e)
                        e.tensor_mul(pool[11].ap[0:1, hsl],
                                     pool[12].ap[0:1, hsl],
                                     pool[12].ap[0:1, hsl])
                        bi = e.tensor_sub(pool[10].ap[0:1, hsl],
                                          pool[10].ap[0:1, hsl],
                                          pool[11].ap[0:1, hsl])
                        tok = s_v.inc(bi)
                        pool[11].consumed(s_v, tok)
                        pool[12].consumed(s_v, tok)
                        pool[10].produced(s_v, tok)
                V_(fvx)
                if phase == 1:
                    def frs(e, jh=jh):
                        pool[10].wait_ready(e)
                        cst_sb.wait_ready(e)
                        hsl = slice(jh * 2 * TCW, (jh + 1) * 2 * TCW)
                        s1 = pool[10].ap[0:1, hsl]
                        e.activation(s1, s1, AF.Ln, bias=cst_sb.ap[0:1, 0:1])
                        if jh == 0:
                            srr.acquire(e)
                        bi = e.activation(srr.ap[:, hsl], s1, AF.Exp,
                                          scale=cst_sb.ap[0:1, 2:3])
                        tok = s_a.inc(bi)
                        pool[10].consumed(s_a, tok)
                        cst_sb.consumed(s_a, tok)
                        srr.produced(s_a, tok)
                    A_(frs)
            yield
        def fmus(e, b=b):
            pool[12].wait_ready(e)
            bi = e.dma_start(out=mu_d[b, 0:1, :],
                             in_=pool[12].ap[0:1, :].bitcast(F32R))
            s = g_st.nxt()
            tok = s.inc(bi, 16)
            pool[12].consumed(s, tok)
            store_toks[("mu", b)] = (s, tok)
            srr.wait_ready(e)
            bi = e.dma_start(out=mu_d[b, 1:2, :], in_=srr.ap)
            s = g_st.nxt()
            tok = s.inc(bi, 16)
            srr.consumed(s, tok)
            store_toks[("rr", b)] = (s, tok)
        G_(fmus)
        for half, key, dstb in ((0, "mu", pool[10]), (1, "rr", pool[12])):
            def fbl(e, half=half, key=key, dstb=dstb, b=b):
                dstb.acquire(e)
                ss, tt = store_toks[(key, b)]
                Buf._wait(e, ss, tt)
                bi = e.dma_start(out=dstb.ap.bitcast(F32R),
                                 in_=mu_d[b, half:half + 1, :]
                                 .broadcast_to([128, T]))
                s = g_st.nxt()
                dstb.produced(s, s.inc(bi, 16))
            G_(fbl)
        yield
        for k in range(KD):
            def fap(e, k=k):
                pool[10].wait_ready(e)
                pool[12].wait_ready(e)
                pool[11].acquire(e)
                e.tensor_sub(pool[11].ap, pool[k].ap, pool[10].ap)
                bi = e.tensor_mul(pool[k].ap, pool[11].ap, pool[12].ap)
                tok = s_v.inc(bi)
                pool[10].consumed(s_v, tok)
                pool[12].consumed(s_v, tok)
                pool[11].consumed(s_v, tok)
                pool[k].produced(s_v, tok)
            V_(fap)
            store(pool[k], xn_d[b, k])
            yield

    def ffn_stage(b, w_ap, in_d, inname, out_d, outname, is_last, in_r):
        pcnt = [0]
        for j in range(NTC):
            for kh in range(2):
                def fld(e, j=j, kh=kh):
                    ld_r[kh].acquire(e)
                    for q in range(4):
                        k = 4 * kh + q
                        rawkey = (inname, b, k, j)
                        if rawkey in store_toks:
                            ss, tt = store_toks[rawkey]
                            Buf._wait(e, ss, tt)
                        bi = e.dma_start(
                            out=ld_r[kh].ap[:, q * TCW:(q + 1) * TCW],
                            in_=in_d[b, k, :, j * TCW:(j + 1) * TCW])
                        s = (g_ld if in_r else g_st).nxt()
                        ld_r[kh].produced(s, s.inc(bi, 16))
                if in_r:
                    L_(fld)
                else:
                    G_(fld)
            for m in range(KD):
                wslot = wt_f[m % 2]

                def fw(e, m=m, wslot=wslot):
                    wslot.acquire(e)
                    bi = e.dma_start(out=wslot.ap, in_=w_ap[m])
                    s = g_ld.nxt()
                    wslot.produced(s, s.inc(bi, 16))
                L_(fw)

                bank = ps_b[pcnt[0] % 2]
                pcnt[0] += 1

                def ftk(e, m=m, wslot=wslot, bank=bank):
                    wslot.wait_ready(e)
                    ld_r[0].wait_ready(e)
                    ld_r[1].wait_ready(e)
                    bank.acquire(e)
                    tok = None
                    for k in range(KD):
                        bi = e.matmul(bank.ap,
                                      wslot.ap[:, k * 128:(k + 1) * 128],
                                      ld_r[k // 4].ap[:, (k % 4) * TCW:
                                                      (k % 4 + 1) * TCW],
                                      start=(k == 0), stop=(k == KD - 1))
                        tok = s_t.inc(bi)
                    wslot.consumed(s_t, tok)
                    ld_r[0].consumed(s_t, tok)
                    ld_r[1].consumed(s_t, tok)
                    bank.produced(s_t, tok)
                T_(ftk)

                slot = pool[8 + m % 2]

                def fe(e, m=m, slot=slot, bank=bank):
                    slot.acquire(e)
                    bank.wait_ready(e)
                    if is_last:
                        ffnb_sb.wait_ready(e)
                        bi = e.activation(slot.ap[:, 0:TCW], bank.ap, AF.Tanh,
                                          bias=ffnb_sb.ap[:, m:m + 1])
                        tok = s_a.inc(bi)
                        ffnb_sb.consumed(s_a, tok)
                    else:
                        bi = e.activation(slot.ap[:, 0:TCW].bitcast(F32R),
                                          bank.ap, AF.Copy)
                        tok = s_a.inc(bi)
                    bank.consumed(s_a, tok)
                    slot.produced(s_a, tok)
                A_(fe)
                store(slot, out_d[b, m, :, j * TCW:(j + 1) * TCW],
                      src_ap=(slot.ap[:, 0:TCW] if is_last
                              else slot.ap[:, 0:TCW].bitcast(F32R)),
                      key=(outname, b, m, j))
                yield

    def ffn(b):
        loads_barrier()
        yield from ffn_stage(b, ffnw, xn_d, "xn", h1_d, "h1", False, False)
        yield from ffn_stage(b, ffna1, h1_d, "h1", h2_d, "h2", False, True)
        yield from ffn_stage(b, ffna2, h2_d, "h2", ffn_d, "ffn", True, True)

    def load_layer_weights(l):
        load(xpwT_sb, W[f"xpwT{l}"])
        load(convw_sb, W[f"convw{l}"])
        load(convb_sb, W[f"convb{l}"])
        load(dtb_sb, W[f"dtb{l}"])
        load(dsk_sb, W[f"dsk{l}"])
        load(lnw_sb, W[f"lnw{l}"])
        load(lnb_sb, W[f"lnb{l}"])
        load(A_sb, W[f"alog{l}"])

        def fa(e):
            A_sb.wait_ready(e)
            bi = e.activation(A_sb.ap, A_sb.ap, AF.Exp)
            A_sb.produced(s_a, s_a.inc(bi))
        A_(fa)

        def fv(e):
            A_sb.wait_ready(e)
            bi = e.tensor_scalar(A_sb.ap, A_sb.ap, -1.0, None, OP.mult)
            A_sb.produced(s_v, s_v.inc(bi))
        V_(fv)

    def gen_A(l, b, stream_d):
        loads_barrier()
        bxc = Buf(pool[11].ap[:, 0:515])
        bcv = Buf(pool[11].ap[:, 1024:1536])
        buz = Buf(pool[11].ap[:, 1536:2048])
        pool[11].cons = {}
        jm = [0]
        for j in range(NTC):
            for kh in range(2):
                def fls(e, j=j, kh=kh):
                    pool[kh].acquire(e)
                    for q in range(4):
                        bi = e.dma_start(
                            out=pool[kh].ap[:, q * TCW:(q + 1) * TCW],
                            in_=stream_d[b, 4 * kh + q, :,
                                         j * TCW:(j + 1) * TCW])
                        s = g_ld.nxt()
                        pool[kh].produced(s, s.inc(bi, 16))
                L_(fls)
            for m in range(2 * KI):
                dtile = m % KI
                is_z = m >= KI
                wslot = wt_in[m % 2]

                def fw(e, mm_=None):
                    wslot_ = wt_in[mm_ % 2]
                    wslot_.acquire(e)
                    bi = e.dma_start(out=wslot_.ap, in_=W[f"inwT{l}"][mm_])
                    s = g_ld.nxt()
                    wslot_.produced(s, s.inc(bi, 16))
                if m == 0:
                    L_(lambda e: fw(e, mm_=0))
                if m + 1 < 2 * KI:
                    L_(lambda e, m=m: fw(e, mm_=m + 1))

                bank = ps_b[3]
                jm[0] += 1

                def ft(e, j=j, wslot=wslot, bank=bank):
                    wslot.wait_ready(e)
                    pool[0].wait_ready(e)
                    pool[1].wait_ready(e)
                    bank.acquire(e)
                    tok = None
                    for k in range(KD):
                        bi = e.matmul(bank.ap,
                                      wslot.ap[:, k * 128:(k + 1) * 128],
                                      pool[k // 4].ap[:, (k % 4) * TCW:
                                                      (k % 4 + 1) * TCW],
                                      start=(k == 0), stop=(k == KD - 1))
                        tok = s_t.inc(bi)
                    wslot.consumed(s_t, tok)
                    pool[0].consumed(s_t, tok)
                    pool[1].consumed(s_t, tok)
                    bank.produced(s_t, tok)
                T_(ft)

                if not is_z:
                    def fe(e, bank=bank, dtile=dtile):
                        bxc.acquire(e)
                        bank.wait_ready(e)
                        hist_sb.wait_ready(e)
                        bi = e.activation(pool[11].ap[:, 3:3 + TCW],
                                          bank.ap, AF.Copy)
                        tok = s_a.inc(bi)
                        bank.consumed(s_a, tok)
                        bxc.produced(s_a, tok)
                    A_(fe)

                    def fcv(e, dtile=dtile, j=j):
                        bxc.wait_ready(e)
                        convw_sb.wait_ready(e)
                        bcv.acquire(e)
                        if j == 0:
                            e.memset(pool[11].ap[:, 0:3], 0.0)
                        else:
                            hist_sb.wait_ready(e)
                            e.tensor_copy(pool[11].ap[:, 0:3],
                                          hist_sb.ap[:, dtile * 4:dtile * 4 + 3])
                        e.tensor_scalar(bcv.ap, pool[11].ap[:, 3:3 + TCW],
                                        convw_sb.ap[:, dtile * 4 + 3:dtile * 4 + 4],
                                        None, OP.mult)
                        for tap in (0, 1, 2):
                            sh = 3 - tap
                            bi = e.scalar_tensor_tensor(
                                bcv.ap, pool[11].ap[:, 3 - sh:3 - sh + TCW],
                                convw_sb.ap[:, dtile * 4 + tap:dtile * 4 + tap + 1],
                                bcv.ap, OP.mult, OP.add)
                        hist_sb.acquire(e)
                        bi2 = e.tensor_copy(hist_sb.ap[:, dtile * 4:dtile * 4 + 3],
                                            pool[11].ap[:, TCW:TCW + 3])
                        tok = s_v.inc(bi2)
                        bxc.consumed(s_v, tok)
                        convw_sb.consumed(s_v, tok)
                        bcv.produced(s_v, tok)
                        hist_sb.produced(s_v, tok)
                    V_(fcv)

                    def fu(e, dtile=dtile, j=j):
                        bcv.wait_ready(e)
                        convb_sb.wait_ready(e)
                        buz.acquire(e)
                        bi = e.activation(buz.ap, bcv.ap, AF.Silu,
                                          bias=convb_sb.ap[:, dtile:dtile + 1])
                        tok = s_a.inc(bi)
                        buz.produced(s_a, tok)
                        u_r.acquire(e)
                        bi = e.activation(u_r.ap[:, j * TCW:(j + 1) * TCW],
                                          bcv.ap, AF.Silu,
                                          bias=convb_sb.ap[:, dtile:dtile + 1])
                        tok = s_a.inc(bi)
                        bcv.consumed(s_a, tok)
                        convb_sb.consumed(s_a, tok)
                        u_r.produced(s_a, tok)
                    A_(fu)
                    store(buz, u_d[b, dtile, :, j * TCW:(j + 1) * TCW])

                    def fxp(e, dtile=dtile, j=j):
                        u_r.wait_ready(e)
                        xpwT_sb.wait_ready(e)
                        if dtile == 0 and j == 0:
                            ps_a.acquire(e)
                        bi = e.matmul(ps_a.ap[0:96, j * TCW:(j + 1) * TCW],
                                      xpwT_sb.ap[:, dtile * 96:(dtile + 1) * 96],
                                      u_r.ap[:, j * TCW:(j + 1) * TCW],
                                      start=(dtile == 0), stop=(dtile == KI - 1))
                        tok = s_t.inc(bi)
                        u_r.consumed(s_t, tok)
                        xpwT_sb.consumed(s_t, tok)
                        if dtile == KI - 1 and j == NTC - 1:
                            ps_a.produced(s_t, tok)
                    T_(fxp)
                else:
                    def fz(e, bank=bank, dtile=dtile, j=j):
                        buz.acquire(e)
                        bank.wait_ready(e)
                        bi = e.activation(buz.ap, bank.ap, AF.Silu)
                        tok = s_a.inc(bi)
                        bank.consumed(s_a, tok)
                        buz.produced(s_a, tok)
                    A_(fz)
                    store(buz, zs_d[b, dtile, :, j * TCW:(j + 1) * TCW])
                yield

        def fxe(e, b=b):
            ps_a.wait_ready(e)
            xdbl[b].acquire(e)
            bi = e.activation(xdbl[b].ap, ps_a.ap[0:96, :], AF.Copy)
            tok = s_a.inc(bi)
            ps_a.consumed(s_a, tok)
            xdbl[b].produced(s_a, tok)
        A_(fxe)

        def fxs(e, b=b):
            xdbl[b].wait_ready(e)
            bi = e.dma_start(out=xdbl_d[b], in_=xdbl[b].ap)
            s = g_st.nxt()
            tok = s.inc(bi, 16)
            xdbl[b].consumed(s, tok)
            store_toks[("xdbl", b)] = (s, tok)
        G_(fxs)
        yield

    def gen_scan(l, b):
        loads_barrier()
        for g in range(NGRP):
            for i in range(G):
                dtile = g * G + i
                load(dtw1, W[f"dtwT{l}"][:, dtile * 128:(dtile + 1) * 128])
                for j in range(NTC):
                    bank = ps_b[2]

                    def fdt(e, dtile=dtile, j=j, bank=bank, i=i):
                        dtw1.wait_ready(e)
                        xdbl[b].wait_ready(e)
                        bank.acquire(e)
                        bi = e.matmul(bank.ap, dtw1.ap,
                                      xdbl[b].ap[0:64, j * TCW:(j + 1) * TCW],
                                      start=True, stop=True)
                        tok = s_t.inc(bi)
                        xdbl[b].consumed(s_t, tok)
                        dtw1.consumed(s_t, tok)
                        bank.produced(s_t, tok)
                    T_(fdt)

                    def fspx(e, dtile=dtile, j=j, bank=bank, i=i):
                        if j == 0:
                            pool[4 + i].acquire(e)
                        bank.wait_ready(e)
                        dtb_sb.wait_ready(e)
                        bi = e.activation(pool[4 + i].ap[:, j * TCW:(j + 1) * TCW],
                                          bank.ap, AF.Exp,
                                          bias=dtb_sb.ap[:, dtile:dtile + 1])
                        tok = s_a.inc(bi)
                        bank.consumed(s_a, tok)
                        dtb_sb.consumed(s_a, tok)
                        if j == NTC - 1:
                            bi = e.activation(pool[4 + i].ap, pool[4 + i].ap,
                                              AF.Ln, bias=cst_sb.ap[:, 1:2])
                            tok = s_a.inc(bi)
                            cst_sb.consumed(s_a, tok)
                        pool[4 + i].produced(s_a, tok)
                    A_(fspx)

                uslot = pool[9] if i == 0 else pool[12]
                load(uslot, u_d[b, dtile])

                def fdtu(e, i=i, uslot=uslot):
                    pool[4 + i].wait_ready(e)
                    uslot.wait_ready(e)
                    pool[6 + i].acquire(e)
                    bi = e.tensor_mul(pool[6 + i].ap, pool[4 + i].ap, uslot.ap)
                    tok = s_v.inc(bi)
                    pool[4 + i].consumed(s_v, tok)
                    uslot.consumed(s_v, tok)
                    pool[6 + i].produced(s_v, tok)
                V_(fdtu)
                yield

            def em_bload(n, half):
                dstb = ld_r[half]
                row = (64 if half == 0 else 80) + n
                def fbl(e, dstb=dstb, row=row):
                    dstb.acquire(e)
                    ss, tt = store_toks[("xdbl", b)]
                    Buf._wait(e, ss, tt)
                    bi = e.dma_start(out=dstb.ap,
                                     in_=xdbl_d[b, row:row + 1, :]
                                     .broadcast_to([128, T]))
                    s = g_ld.nxt()
                    dstb.produced(s, s.inc(bi, 16))
                L_(fbl)

            def em_da(n, i):
                dtile = g * G + i
                ci = dtile * NST + n
                def fda(e, i=i, ci=ci):
                    pool[8].acquire(e)
                    pool[4 + i].wait_ready(e)
                    A_sb.wait_ready(e)
                    bi = e.activation(pool[8].ap, pool[4 + i].ap, AF.Exp,
                                      scale=A_sb.ap[:, ci:ci + 1])
                    tok = s_a.inc(bi)
                    pool[4 + i].consumed(s_a, tok)
                    A_sb.consumed(s_a, tok)
                    pool[8].produced(s_a, tok)
                A_(fda)

            def em_dve(n, i):
                pslot = pool[9] if (n * G + i) % 2 == 0 else pool[12]
                def fscan(e, i=i, n=n, pslot=pslot):
                    pslot.acquire(e)
                    ld_r[0].wait_ready(e)
                    pool[6 + i].wait_ready(e)
                    Bb32 = ld_r[0].ap.bitcast(F32)
                    bi = e.tensor_mul(pslot.ap, pool[6 + i].ap, Bb32)
                    tok = s_v.inc(bi)
                    ld_r[0].consumed(s_v, tok)
                    pool[6 + i].consumed(s_v, tok)
                    pslot.produced(s_v, tok)
                    pool[8].wait_ready(e)
                    pool[10].acquire(e)
                    bi = e.tensor_tensor_scan(pool[10].ap, pool[8].ap,
                                              pslot.ap, 0.0, OP.mult, OP.add)
                    tok = s_v.inc(bi)
                    pool[8].consumed(s_v, tok)
                    pslot.consumed(s_v, tok)
                    pool[10].produced(s_v, tok)
                    ld_r[1].wait_ready(e)
                    Cb32 = ld_r[1].ap.bitcast(F32)
                    if n == 0:
                        pool[2 + i].acquire(e)
                        bi = e.tensor_mul(pool[2 + i].ap, pool[10].ap, Cb32)
                        tok = s_v.inc(bi)
                        pool[2 + i].produced(s_v, tok)
                    else:
                        bi = e.tensor_mul(pslot.ap, pool[10].ap, Cb32)
                        tok = s_v.inc(bi)
                        pslot.produced(s_v, tok)
                    ld_r[1].consumed(s_v, tok)
                    pool[10].consumed(s_v, tok)
                V_(fscan)
                if n > 0:
                    def fadd(e, i=i, pslot=pslot):
                        pslot.wait_ready(e)
                        pool[2 + i].wait_ready(e)
                        bi = e.tensor_add(pool[2 + i].ap, pool[2 + i].ap,
                                          pslot.ap)
                        tok = s_v.inc(bi)
                        pslot.consumed(s_v, tok)
                        pool[2 + i].produced(s_v, tok)
                    V_(fadd)

            em_bload(0, 0)
            em_bload(0, 1)
            em_da(0, 0)
            for n in range(NST):
                em_dve(n, 0)
                em_da(n, 1)
                em_dve(n, 1)
                if n + 1 < NST:
                    em_bload(n + 1, 0)
                    em_da(n + 1, 0)
                    em_bload(n + 1, 1)
                yield

            for i in range(G):
                dtile = g * G + i
                zslot = pool[9]
                uslot = pool[12]
                load(zslot, zs_d[b, dtile])
                load(uslot, u_d[b, dtile])

                def fg(e, i=i, dtile=dtile, zslot=zslot, uslot=uslot):
                    pool[8].acquire(e)
                    uslot.wait_ready(e)
                    pool[2 + i].wait_ready(e)
                    dsk_sb.wait_ready(e)
                    bi = e.scalar_tensor_tensor(
                        pool[8].ap, uslot.ap,
                        dsk_sb.ap[:, dtile:dtile + 1], pool[2 + i].ap,
                        OP.mult, OP.add)
                    tok = s_v.inc(bi)
                    uslot.consumed(s_v, tok)
                    pool[2 + i].consumed(s_v, tok)
                    dsk_sb.consumed(s_v, tok)
                    pool[8].produced(s_v, tok)
                    zslot.wait_ready(e)
                    pool[10].acquire(e)
                    bi = e.tensor_mul(pool[10].ap.bitcast(F32R), pool[8].ap,
                                      zslot.ap)
                    tok = s_v.inc(bi)
                    pool[8].consumed(s_v, tok)
                    zslot.consumed(s_v, tok)
                    pool[10].produced(s_v, tok)
                V_(fg)
                store(pool[10], yp_d[b, dtile],
                      src_ap=pool[10].ap.bitcast(F32R))
                yield

    def gen_out_mm(l, b):
        loads_barrier()
        for m in range(KD):
            wslot = wt_out[m % 2]

            def fw(e, m=m, wslot=wslot):
                wslot.acquire(e)
                bi = e.dma_start(out=wslot.ap, in_=W[f"outwT{l}"][m])
                s = g_ld.nxt()
                wslot.produced(s, s.inc(bi, 16))
            L_(fw)

            for k in range(KI):
                load(u_r, yp_d[b, k])

                def ftk(e, k=k, wslot=wslot):
                    wslot.wait_ready(e)
                    u_r.wait_ready(e)
                    if k == 0:
                        ps_a.acquire(e)
                    tok = None
                    for j in range(NTC):
                        bi = e.matmul(ps_a.ap[:, j * TCW:(j + 1) * TCW],
                                      wslot.ap[:, k * 128:(k + 1) * 128],
                                      u_r.ap[:, j * TCW:(j + 1) * TCW],
                                      start=(k == 0), stop=(k == KI - 1))
                        tok = s_t.inc(bi)
                    u_r.consumed(s_t, tok)
                    wslot.consumed(s_t, tok)
                    if k == KI - 1:
                        ps_a.produced(s_t, tok)
                T_(ftk)
                yield

            def fe(e, m=m):
                pool[11].acquire(e)
                ps_a.wait_ready(e)
                bi = e.activation(pool[11].ap.bitcast(F32R), ps_a.ap, AF.Copy)
                tok = s_a.inc(bi)
                ps_a.consumed(s_a, tok)
                pool[11].produced(s_a, tok)
            A_(fe)
            store(pool[11], h1_d[b, m], src_ap=pool[11].ap.bitcast(F32R),
                  key=("o", b, m))
            yield

    def gen_ln(l, b, stream_d, next_d):
        otiles = [pool[2 + kk] for kk in range(KD)]
        for k in range(KD):
            def flo(e, k=k):
                otiles[k].acquire(e)
                if ("o", b, k) in store_toks:
                    ss, tt = store_toks[("o", b, k)]
                    Buf._wait(e, ss, tt)
                bi = e.dma_start(out=otiles[k].ap.bitcast(F32R),
                                 in_=h1_d[b, k])
                s = g_ld.nxt()
                otiles[k].produced(s, s.inc(bi, 16))
            L_(flo)
        smu_ap = ld_r[0].ap[0:1, :]
        def fvinit(e):
            ld_r[0].acquire(e)
            pool[12].acquire(e)
        V_(fvinit)
        for phase in (0, 1):
            for jh in (0, 1):
                for k in range(KD):
                    if phase == 1:
                        def fsq(e, k=k):
                            pool[10].acquire(e)
                            otiles[k].wait_ready(e)
                            bi = e.activation(pool[10].ap, otiles[k].ap,
                                              AF.Square)
                            tok = s_a.inc(bi)
                            pool[10].produced(s_a, tok)
                            otiles[k].consumed(s_a, tok)
                        A_(fsq)

                    def fst(e, k=k, jh=jh, phase=phase):
                        srcb = otiles[k] if phase == 0 else pool[10]
                        srcb.wait_ready(e)
                        tok = None
                        for jj in (0, 1):
                            j = 2 * jh + jj
                            if k == 0:
                                ps_b[jj].acquire(e)
                            bi = e.matmul(ps_b[jj].ap[0:1, :], onesc_sb.ap,
                                          srcb.ap[:, j * TCW:(j + 1) * TCW],
                                          start=(k == 0), stop=(k == KD - 1))
                            tok = s_t.inc(bi)
                        srcb.consumed(s_t, tok)
                        if k == KD - 1:
                            ps_b[0].produced(s_t, tok)
                            ps_b[1].produced(s_t, tok)
                    T_(fst)

                def fvx(e, jh=jh, phase=phase):
                    ps_b[0].wait_ready(e)
                    ps_b[1].wait_ready(e)
                    mu2 = pool[12].ap[0:1, :]
                    tok = None
                    for jj in (0, 1):
                        j = 2 * jh + jj
                        sl = slice(j * TCW, (j + 1) * TCW)
                        if phase == 0:
                            e.tensor_scalar(smu_ap[:, sl], ps_b[jj].ap[0:1, :],
                                            1.0 / 1024.0, None, OP.mult)
                            bi = e.tensor_scalar(mu2[:, sl],
                                                 ps_b[jj].ap[0:1, :],
                                                 1.0 / 1024.0, None, OP.mult)
                        else:
                            if jj == 0:
                                pool[10].acquire(e)
                            bi = e.tensor_scalar(pool[10].ap[0:1, sl],
                                                 ps_b[jj].ap[0:1, :],
                                                 1.0 / 1024.0, None, OP.mult)
                        tok = s_v.inc(bi)
                        ps_b[jj].consumed(s_v, tok)
                    if phase == 0 and jh == 1:
                        bi = e.tensor_mul(smu_ap, mu2, mu2)
                        tok = s_v.inc(bi)
                        ld_r[0].produced(s_v, tok)
                        pool[12].produced(s_v, tok)
                    if phase == 1:
                        hsl = slice(jh * 2 * TCW, (jh + 1) * 2 * TCW)
                        ld_r[0].wait_ready(e)
                        bi = e.tensor_sub(pool[10].ap[0:1, hsl],
                                          pool[10].ap[0:1, hsl],
                                          smu_ap[:, hsl].bitcast(F32))
                        tok = s_v.inc(bi)
                        ld_r[0].consumed(s_v, tok)
                        pool[10].produced(s_v, tok)
                V_(fvx)
                if phase == 1:
                    def frs(e, jh=jh):
                        pool[10].wait_ready(e)
                        cst_sb.wait_ready(e)
                        hsl = slice(jh * 2 * TCW, (jh + 1) * 2 * TCW)
                        s1 = pool[10].ap[0:1, hsl]
                        e.activation(s1, s1, AF.Ln, bias=cst_sb.ap[0:1, 0:1])
                        if jh == 0:
                            srr.acquire(e)
                        bi = e.activation(srr.ap[:, hsl], s1, AF.Exp,
                                          scale=cst_sb.ap[0:1, 2:3])
                        tok = s_a.inc(bi)
                        pool[10].consumed(s_a, tok)
                        cst_sb.consumed(s_a, tok)
                        srr.produced(s_a, tok)
                    A_(frs)
            yield
        def fmus(e):
            pool[12].wait_ready(e)
            bi = e.dma_start(out=mu_d[b, 0:1, :],
                             in_=pool[12].ap[0:1, :].bitcast(F32R))
            s = g_st.nxt()
            tok = s.inc(bi, 16)
            pool[12].consumed(s, tok)
            store_toks[("mu", b)] = (s, tok)
            srr.wait_ready(e)
            bi = e.dma_start(out=mu_d[b, 1:2, :], in_=srr.ap)
            s = g_st.nxt()
            tok = s.inc(bi, 16)
            srr.consumed(s, tok)
            store_toks[("rr", b)] = (s, tok)
        G_(fmus)

        for half, key in ((0, "mu"), (1, "rr")):
            def fbl(e, half=half, key=key):
                ld_r[half].acquire(e)
                ss, tt = store_toks[(key, b)]
                Buf._wait(e, ss, tt)
                bi = e.dma_start(out=ld_r[half].ap,
                                 in_=mu_d[b, half:half + 1, :]
                                 .broadcast_to([128, T]))
                s = g_st.nxt()
                ld_r[half].produced(s, s.inc(bi, 16))
            G_(fbl)
        yield

        for k in range(KD):
            def fap(e, k=k):
                ld_r[0].wait_ready(e)
                ld_r[1].wait_ready(e)
                if k == 0:
                    pool[10].acquire(e)
                lnw_sb.wait_ready(e)
                lnb_sb.wait_ready(e)
                mub = ld_r[0].ap.bitcast(F32)
                rrb = ld_r[1].ap.bitcast(F32)
                e.tensor_sub(pool[10].ap, otiles[k].ap, mub)
                e.tensor_mul(pool[10].ap, pool[10].ap, rrb)
                bi = e.tensor_scalar(otiles[k].ap, pool[10].ap,
                                     lnw_sb.ap[:, k:k + 1],
                                     lnb_sb.ap[:, k:k + 1],
                                     OP.mult, OP.add)
                tok = s_v.inc(bi)
                ld_r[0].consumed(s_v, tok)
                ld_r[1].consumed(s_v, tok)
                lnw_sb.consumed(s_v, tok)
                lnb_sb.consumed(s_v, tok)
                otiles[k].produced(s_v, tok)
            V_(fap)
        yield
        for k in range(KD):
            load(pool[12], stream_d[b, k])

            def fres(e, k=k):
                pool[12].wait_ready(e)
                otiles[k].wait_ready(e)
                pool[10].acquire(e)
                bi = e.tensor_add(pool[10].ap, otiles[k].ap, pool[12].ap)
                tok = s_v.inc(bi)
                pool[12].consumed(s_v, tok)
                otiles[k].consumed(s_v, tok)
                pool[10].produced(s_v, tok)
            V_(fres)
            store(pool[10], next_d[b, k])
            yield

    def run_gen(gen):
        for _ in gen:
            pass

    def interleave(g1, g2):
        done1 = done2 = False
        while not (done1 and done2):
            if not done1:
                try:
                    next(g1)
                except StopIteration:
                    done1 = True
            if not done2:
                try:
                    next(g2)
                except StopIteration:
                    done2 = True

    def final(b):
        loads_barrier()
        for k in range(KD):
            base = (k % 2) * 4
            i0, i1, i2, i3 = base, base + 1, base + 2, base + 3
            load(pool[i0], xfm[b, k])
            load(pool[i1], m2_d[b, k])
            load(pool[i2], ffn_d[b, k])

            def fv(e, i0=i0, i1=i1, i2=i2, i3=i3):
                pool[i0].wait_ready(e)
                pool[i1].wait_ready(e)
                pool[i2].wait_ready(e)
                pool[i3].acquire(e)
                e.tensor_add(pool[i3].ap, pool[i0].ap, pool[i1].ap)
                bi = e.tensor_add(pool[i3].ap, pool[i3].ap, pool[i2].ap)
                tok = s_v.inc(bi)
                pool[i0].consumed(s_v, tok)
                pool[i1].consumed(s_v, tok)
                pool[i2].consumed(s_v, tok)
                pool[i3].produced(s_v, tok)
            V_(fv)
            store(pool[i3], outfm[b, k])

    run_gen(p0_ln(0))
    flush()
    interleave(p0_ln(1), ffn(0))
    flush()
    ffn1_pending = ffn(1)
    _NL = int(_os.environ.get("KD_NLAYERS", "2"))
    pending_ln = None
    for l in range(_NL):
        stream_d = xn_d if l == 0 else m1_d
        next_d = m1_d if l == 0 else m2_d
        load_layer_weights(l)
        flush()
        if pending_ln is not None:
            interleave(pending_ln, gen_A(l, 0, stream_d))
            pending_ln = None
        elif ffn1_pending is not None:
            interleave(ffn1_pending, gen_A(l, 0, stream_d))
            ffn1_pending = None
        else:
            run_gen(gen_A(l, 0, stream_d))
        flush()
        interleave(gen_scan(l, 0), gen_A(l, 1, stream_d))
        flush()
        interleave(gen_scan(l, 1), gen_out_mm(l, 0))
        flush()
        interleave(gen_ln(l, 0, stream_d, next_d), gen_out_mm(l, 1))
        flush()
        pending_ln = gen_ln(l, 1, stream_d, next_d)
    if pending_ln is not None:
        run_gen(pending_ln)
        flush()
    for b in range(BC):
        final(b)
    flush()
    def f_fin_g(e):
        for s in g_st.sems:
            e.wait_ge(s.h, s.n)
    G_(f_fin_g)
    def f_fin_l(e):
        for s in g_ld.sems:
            e.wait_ge(s.h, s.n)
    L_(f_fin_l)
    def f_fin_a(e):
        for s in g_sc.sems:
            e.wait_ge(s.h, s.n)
    A_(f_fin_a)

    flush()

    st.close()
    nc._kd_trace = trace_log
    return nc


def _ffn_tile(w):
    w = np.asarray(w).reshape(8, 128, 8, 128)
    return np.ascontiguousarray(w.transpose(2, 1, 0, 3)).reshape(8, 128, 1024)


def host_inputs(inputs, core):
    asc = np.ascontiguousarray
    x = np.asarray(inputs["L0_attn_x"])[2 * core:2 * core + 2]
    d = {
        "xfm": asc(x.transpose(0, 2, 1)).reshape(BC, KD, 128, T),
        "ffnw": _ffn_tile(inputs["L0_ffn_weight"]),
        "ffna1": _ffn_tile(inputs["L0_ffn_aux_weight1"]),
        "ffna2": _ffn_tile(inputs["L0_ffn_aux_weight2"]),
        "ffnb": asc(np.asarray(inputs["L0_ffn_bias"]).reshape(8, 128).T),
        "onesr": np.ones((1, 128), np.float32),
        "onesc": np.ones((128, 1), np.float32),
        "cst": np.repeat(np.array([[EPS, 1.0, -0.5, 0.0]], np.float32), 128, axis=0),
    }
    for l in (0, 1):
        inwT = np.asarray(inputs["L0_attn_in_proj_w"])[l].T
        d[f"inwT{l}"] = asc(inwT.reshape(8, 128, 32, 128)
                            .transpose(2, 1, 0, 3)).reshape(32, 128, 8 * 128)
        outwT = np.asarray(inputs["L0_attn_out_proj_w"])[l].T
        d[f"outwT{l}"] = asc(outwT.reshape(16, 128, 8, 128)
                             .transpose(2, 1, 0, 3)).reshape(8, 128, 16 * 128)
        d[f"dtwT{l}"] = asc(np.asarray(inputs["L0_attn_dt_proj_w"])[l].T)
        d[f"xpwT{l}"] = asc(np.asarray(inputs["L0_attn_x_proj_w"])[l].T
                            .reshape(16, 128, 96).transpose(1, 0, 2)).reshape(128, 16 * 96)
        d[f"convw{l}"] = asc(np.asarray(inputs["L0_attn_conv_w"])[l]
                             .reshape(16, 128, 4).transpose(1, 0, 2)).reshape(128, 64)
        d[f"convb{l}"] = asc(np.asarray(inputs["L0_attn_conv_b"])[l].reshape(16, 128).T)
        d[f"dtb{l}"] = asc(np.asarray(inputs["L0_attn_dt_proj_b"])[l].reshape(16, 128).T)
        d[f"dsk{l}"] = asc(np.asarray(inputs["L0_attn_Dskip"])[l].reshape(16, 128).T)
        d[f"alog{l}"] = asc(np.asarray(inputs["L0_attn_A_log"])[l]
                            .reshape(16, 128, 16).transpose(1, 0, 2)).reshape(128, 256)
        d[f"lnw{l}"] = asc(np.asarray(inputs["L0_attn_ln_w"])[l].reshape(8, 128).T)
        d[f"lnb{l}"] = asc(np.asarray(inputs["L0_attn_ln_b"])[l].reshape(8, 128).T)
    return {k: np.ascontiguousarray(v, dtype=np.float32) for k, v in d.items()}


def kernel(**inputs):
    nc = build_nc()
    in_maps = [host_inputs(inputs, c) for c in range(NCORE)]
    res = run_bass_kernel_spmd(nc, in_maps, core_ids=list(range(NCORE)))
    outs = []
    for c in range(NCORE):
        o = np.asarray(res.results[c]["outfm"])
        o = o.reshape(BC, 1024, T).transpose(0, 2, 1)
        outs.append(o)
    return np.ascontiguousarray(np.concatenate(outs, axis=0), dtype=np.float32)


# version 13 (final)
# speedup vs baseline: 1.0906x; 1.0008x over previous
import numpy as np
from contextlib import ExitStack
import concourse.bass as bass
import concourse.mybir as mybir
from concourse.bass_utils import run_bass_kernel_spmd

F32 = mybir.dt.float32
F32R = mybir.dt.float32r
AF = mybir.ActivationFunctionType
OP = mybir.AluOpType

NCORE = 8
BC = 2
T = 2048
KD = 8
KI = 16
NST = 16
TCW = 512
NTC = T // TCW
G = 2
NGRP = KI // G
EPS = 1e-5


class Sem:
    def __init__(self, st, nc, name):
        self.h = st.enter_context(nc.semaphore(name))
        self.n = 0

    def inc(self, bi, amt=1):
        bi.then_inc(self.h, amt)
        self.n += amt
        return self.n


class SemGroup:
    def __init__(self, st, nc, name, k):
        self.sems = [Sem(st, nc, f"{name}{i}") for i in range(k)]
        self.i = 0

    def nxt(self):
        s = self.sems[self.i]
        self.i = (self.i + 1) % len(self.sems)
        return s


class Buf:
    CUR = ['?']
    WAITED = {}

    @classmethod
    def _wait(cls, eng, s, t):
        key = (cls.CUR[0], id(s))
        if cls.WAITED.get(key, -1) >= t:
            return
        eng.wait_ge(s.h, t)
        cls.WAITED[key] = t

    def __init__(self, ap):
        self.ap = ap
        self.ready = {}
        self.cons = {}

    def acquire(self, eng):
        for s, t in self.cons.items():
            Buf._wait(eng, s, t)
        self.cons = {}
        self.ready = {}

    def produced(self, s, tok):
        if s in self.ready:
            self.ready[s] = max(self.ready[s], tok)
        else:
            self.ready[s] = tok

    def wait_ready(self, eng):
        for s, t in self.ready.items():
            Buf._wait(eng, s, t)

    def consumed(self, s, tok):
        if s in self.cons:
            self.cons[s] = max(self.cons[s], tok)
        else:
            self.cons[s] = tok


def build_nc():
    nc = bass.Bass("TRN2", target_bir_lowering=False)

    def din(name, shape):
        return nc.dram_tensor(name, shape, F32, kind="ExternalInput").ap()

    def dout(name, shape):
        return nc.dram_tensor(name, shape, F32, kind="ExternalOutput").ap()

    import os as _os
    _DBG = _os.environ.get("KD_DBG", "0") == "1"

    def dint(name, shape):
        kind = "ExternalOutput" if _DBG else "Internal"
        return nc.dram_tensor(name, shape, F32, kind=kind).ap()

    xfm = din("xfm", [BC, KD, 128, T])
    W = {}
    for l in (0, 1):
        W[f"inwT{l}"] = din(f"inwT{l}", [32, 128, 8 * 128])
        W[f"outwT{l}"] = nc.dram_tensor(f"outwT{l}", [8, 128, 16 * 128], F32R, kind="ExternalInput").ap()
        W[f"dtwT{l}"] = nc.dram_tensor(f"dtwT{l}", [64, 2048], F32R, kind="ExternalInput").ap()
        W[f"xpwT{l}"] = nc.dram_tensor(f"xpwT{l}", [128, 16 * 96], F32R, kind="ExternalInput").ap()
        W[f"convw{l}"] = din(f"convw{l}", [128, 64])
        W[f"convb{l}"] = din(f"convb{l}", [128, 16])
        W[f"dtb{l}"] = din(f"dtb{l}", [128, 16])
        W[f"dsk{l}"] = din(f"dsk{l}", [128, 16])
        W[f"alog{l}"] = din(f"alog{l}", [128, 256])
        W[f"lnw{l}"] = din(f"lnw{l}", [128, 8])
        W[f"lnb{l}"] = din(f"lnb{l}", [128, 8])
    ffnw = nc.dram_tensor("ffnw", [8, 128, 1024], F32R, kind="ExternalInput").ap()
    ffna1 = nc.dram_tensor("ffna1", [8, 128, 1024], F32R, kind="ExternalInput").ap()
    ffna2 = nc.dram_tensor("ffna2", [8, 128, 1024], F32R, kind="ExternalInput").ap()
    ffnb = din("ffnb", [128, 8])
    onesr = din("onesr", [1, 128])
    onesc = din("onesc", [128, 1])
    cst = din("cst", [128, 4])

    outfm = dout("outfm", [BC, KD, 128, T])

    xn_d = dint("xn_d", [BC, KD, 128, T])
    ffn_d = dint("ffn_d", [BC, KD, 128, T])
    h1_d = nc.dram_tensor("h1_d", [BC, KD, 128, T], F32R,
                          kind="ExternalOutput" if _DBG else "Internal").ap()
    h2_d = nc.dram_tensor("h2_d", [BC, KD, 128, T], F32R,
                          kind="ExternalOutput" if _DBG else "Internal").ap()
    m1_d = dint("m1_d", [BC, KD, 128, T])
    m2_d = dint("m2_d", [BC, KD, 128, T])
    u_d = dint("u_d", [BC, KI, 128, T])
    xdbl_d = nc.dram_tensor("xdbl_d", [BC, 96, T], F32R,
                            kind="ExternalOutput" if _DBG else "Internal").ap()
    mu_d = nc.dram_tensor("mu_d", [BC, 2, T], F32R,
                          kind="ExternalOutput" if _DBG else "Internal").ap()
    zs_d = dint("zs_d", [BC, KI, 128, T])
    yp_d = nc.dram_tensor("yp_d", [BC, KI, 128, T], F32R,
                          kind="ExternalOutput" if _DBG else "Internal").ap()

    st = ExitStack()
    st.__enter__()

    def mksb(name, shape, dt=F32):
        return Buf(st.enter_context(nc.sbuf_tensor(name, shape, dt)).ap())

    NPOOL = 13
    pool = [mksb(f"pl{i}", [128, T]) for i in range(NPOOL)]
    conv_sb = pool[11]
    xdbl = [mksb(f"xdbl{b}", [96, T], F32R) for b in range(BC)]
    wt_in = [mksb("wt_in0", [128, 1024]), mksb("wt_in1", [128, 1024])]
    wt_f = [mksb("wt_f0", [128, 1024], F32R), mksb("wt_f1", [128, 1024], F32R)]
    ld_r = [mksb("ld_r0", [128, T], F32R), mksb("ld_r1", [128, T], F32R)]
    wt_out = [mksb("wt_out0", [128, 2048], F32R), mksb("wt_out1", [128, 2048], F32R)]
    xpwT_sb = mksb("xpwT_sb", [128, 1536], F32R)
    dtw1 = mksb("dtw1", [64, 128], F32R)
    hist_sb = mksb("hist_sb", [128, 64])
    u_r = mksb("u_r", [128, T], F32R)
    onesr_r = mksb("onesr_r", [1, 128], F32R)
    A_sb = mksb("A_sb", [128, 256])
    convw_sb = mksb("convw_sb", [128, 64])
    convb_sb = mksb("convb_sb", [128, 16])
    dtb_sb = mksb("dtb_sb", [128, 16])
    dsk_sb = mksb("dsk_sb", [128, 16])
    lnw_sb = mksb("lnw_sb", [128, 8])
    lnb_sb = mksb("lnb_sb", [128, 8])
    ffnb_sb = mksb("ffnb_sb", [128, 8])
    onesc_sb = mksb("onesc_sb", [128, 1])
    cst_sb = mksb("cst_sb", [128, 4])
    stgB = mksb("stgB", [1, T], F32R)
    stgC = mksb("stgC", [1, T], F32R)
    srr = stgC

    ps_a = Buf(st.enter_context(nc.psum_tensor("ps_a", [128, T], F32)).ap())
    ps_b = [Buf(st.enter_context(nc.psum_tensor(f"ps_b{j}", [128, TCW], F32)).ap())
            for j in range(NTC)]

    g_ld = SemGroup(st, nc, "s_ld", 8)
    g_st = SemGroup(st, nc, "s_st", 8)
    g_sc = SemGroup(st, nc, "s_sc", 2)
    s_g = Sem(st, nc, "s_g")
    s_t = Sem(st, nc, "s_t")
    s_v = Sem(st, nc, "s_v")
    s_a = Sem(st, nc, "s_a")

    block = st.enter_context(nc.Block())

    Buf.WAITED = {}

    class RecInstr:
        __slots__ = ("incs",)
        def __init__(self):
            self.incs = []
        def then_inc(self, sem, amt):
            self.incs.append((sem, amt))
            return self

    class RecEng:
        __slots__ = ("lst", "kind")
        def __init__(self, lst, kind):
            self.lst = lst
            self.kind = kind
        def __getattr__(self, name):
            lst = self.lst
            kind = self.kind
            def call(*a, **kw):
                ri = RecInstr()
                lst.append((name, a, kw, ri))
                trace_log.append((kind, name, a, kw, ri))
                return ri
            return call

    rec = {k: [] for k in 'tvalg'}
    trace_log = []

    _eng_of = {}

    def flush():
        _eng_of.setdefault('t', block.tensor)
        for k, emit in (('t', block.tensor), ('v', block.vector),
                        ('a', block.scalar), ('l', block.sync),
                        ('g', block.gpsimd)):
            lst = rec[k]
            if not lst:
                continue
            rec[k] = []
            def f(e, lst=lst):
                for name, a, kw, ri in lst:
                    bi = getattr(e, name)(*a, **kw)
                    for sem, amt in ri.incs:
                        bi.then_inc(sem, amt)
            emit(f)

    def T_(fn):
        Buf.CUR[0] = 't'
        fn(RecEng(rec['t'], 't'))

    def V_(fn):
        Buf.CUR[0] = 'v'
        fn(RecEng(rec['v'], 'v'))

    def A_(fn):
        Buf.CUR[0] = 'a'
        fn(RecEng(rec['a'], 'a'))

    def L_(fn):
        Buf.CUR[0] = 'l'
        fn(RecEng(rec['l'], 'l'))

    def G_(fn):
        Buf.CUR[0] = 'g'
        fn(RecEng(rec['g'], 'g'))

    import os
    _PIN = os.environ.get("KD_PIN", "0") == "1"
    _GADD = os.environ.get("KD_GADD", "1") == "1"

    def pin_act(idx):
        if not _PIN:
            return
        def f(e):
            e.add_instruction(mybir.InstLoadActFuncSet(
                name=nc.get_next_instruction_name(),
                act_func_set_id=idx, ins=[], outs=[]))
        A_(f)

    def load(buf, src_ap, dst_ap=None):
        def f(e):
            buf.acquire(e)
            bi = e.dma_start(out=dst_ap if dst_ap is not None else buf.ap,
                             in_=src_ap)
            s = g_ld.nxt()
            buf.produced(s, s.inc(bi, 16))
        L_(f)

    def loadg(buf, src_ap, dst_ap=None):
        def f(e):
            buf.acquire(e)
            bi = e.dma_start(out=dst_ap if dst_ap is not None else buf.ap,
                             in_=src_ap)
            s = g_st.nxt()
            buf.produced(s, s.inc(bi, 16))
        G_(f)

    def store(buf, dst_ap, src_ap=None, key=None):
        def f(e):
            buf.wait_ready(e)
            bi = e.dma_start(out=dst_ap,
                             in_=src_ap if src_ap is not None else buf.ap)
            s = g_st.nxt()
            tok = s.inc(bi, 16)
            buf.consumed(s, tok)
            if key is not None:
                store_toks[key] = (s, tok)
        G_(f)

    store_toks = {}

    def loads_barrier():
        tgts = [(s, s.n) for s in g_st.sems]
        def fl(e):
            for s, t in tgts:
                Buf._wait(e, s, t)
        L_(fl)
        def fg(e):
            for s, t in tgts:
                Buf._wait(e, s, t)
        G_(fg)

    load(onesc_sb, onesc)
    load(ffnb_sb, ffnb)
    load(cst_sb, cst)
    loadg(onesr_r, onesr)

    def ln_stats_and_bcast(tiles, sq_slot, st_slot):
        st1 = st_slot
        for k in range(KD):
            def fa(e, k=k):
                sq_slot.acquire(e)
                tiles[k].wait_ready(e)
                bi = e.activation(sq_slot.ap, tiles[k].ap, AF.Square)
                tok = s_a.inc(bi)
                sq_slot.produced(s_a, tok)
                tiles[k].consumed(s_a, tok)
            A_(fa)

            def ft(e, k=k):
                tiles[k].wait_ready(e)
                sq_slot.wait_ready(e)
                if k == 0:
                    ps_a.acquire(e)
                    for j in range(NTC):
                        ps_b[j].acquire(e)
                tok = None
                for j in range(NTC):
                    e.matmul(ps_a.ap[0:1, j * TCW:(j + 1) * TCW],
                             onesc_sb.ap, tiles[k].ap[:, j * TCW:(j + 1) * TCW],
                             start=(k == 0), stop=(k == KD - 1))
                    bi2 = e.matmul(ps_b[j].ap[0:1, :],
                                   onesc_sb.ap, sq_slot.ap[:, j * TCW:(j + 1) * TCW],
                                   start=(k == 0), stop=(k == KD - 1))
                    tok = s_t.inc(bi2)
                sq_slot.consumed(s_t, tok)
                tiles[k].consumed(s_t, tok)
                if k == KD - 1:
                    ps_a.produced(s_t, tok)
                    for j in range(NTC):
                        ps_b[j].produced(s_t, tok)
            T_(ft)

        def fv(e):
            ps_a.wait_ready(e)
            for j in range(NTC):
                ps_b[j].wait_ready(e)
            ld_r[0].acquire(e)
            smu_ap = ld_r[0].ap[0:1, :]
            st1.acquire(e)
            pool[12].acquire(e)
            mu2 = pool[12].ap[0:1, :]
            s1 = st1.ap[0:1, :]
            e.tensor_scalar(smu_ap, ps_a.ap[0:1, :], 1.0 / 1024.0, None, OP.mult)
            e.tensor_scalar(mu2, ps_a.ap[0:1, :], 1.0 / 1024.0, None, OP.mult)
            for j in range(NTC):
                e.tensor_scalar(s1[:, j * TCW:(j + 1) * TCW],
                                ps_b[j].ap[0:1, :], 1.0 / 1024.0, None, OP.mult)
            e.tensor_mul(mu2, mu2, mu2)
            bi = e.tensor_sub(s1, s1, mu2)
            tok = s_v.inc(bi)
            ps_a.consumed(s_v, tok)
            for j in range(NTC):
                ps_b[j].consumed(s_v, tok)
            pool[12].consumed(s_v, tok)
            ld_r[0].produced(s_v, tok)
            st1.produced(s_v, tok)
        V_(fv)

        def fa2(e):
            st1.wait_ready(e)
            cst_sb.wait_ready(e)
            s1 = st1.ap[0:1, :]
            e.activation(s1, s1, AF.Ln, bias=cst_sb.ap[0:1, 0:1])
            srr.acquire(e)
            bi = e.activation(srr.ap, s1, AF.Exp,
                              scale=cst_sb.ap[0:1, 2:3])
            tok = s_a.inc(bi)
            st1.consumed(s_a, tok)
            srr.produced(s_a, tok)
        A_(fa2)

        def ft2(e):
            ps_a.acquire(e)
            ld_r[0].wait_ready(e)
            smu_ap = ld_r[0].ap[0:1, :]
            srr.wait_ready(e)
            onesr_r.wait_ready(e)
            tok = None
            for j in range(NTC):
                ps_b[j].acquire(e)
                e.matmul(ps_a.ap[:, j * TCW:(j + 1) * TCW], onesr_r.ap,
                         smu_ap[:, j * TCW:(j + 1) * TCW], start=True, stop=True)
                bi2 = e.matmul(ps_b[j].ap, onesr_r.ap,
                               srr.ap[:, j * TCW:(j + 1) * TCW], start=True, stop=True)
                tok = s_t.inc(bi2)
            ld_r[0].consumed(s_t, tok)
            srr.consumed(s_t, tok)
            ps_a.produced(s_t, tok)
            for j in range(NTC):
                ps_b[j].produced(s_t, tok)
        T_(ft2)

    def p0_ln(b):
        for k in range(KD):
            load(pool[k], xfm[b, k])
        yield
        for phase in (0, 1):
            for jh in (0, 1):
                for k in range(KD):
                    if phase == 1:
                        def fsq(e, k=k):
                            pool[11].acquire(e)
                            pool[k].wait_ready(e)
                            bi = e.activation(pool[11].ap, pool[k].ap,
                                              AF.Square)
                            tok = s_a.inc(bi)
                            pool[11].produced(s_a, tok)
                            pool[k].consumed(s_a, tok)
                        A_(fsq)

                    def fst(e, k=k, jh=jh, phase=phase):
                        srcb = pool[k] if phase == 0 else pool[11]
                        srcb.wait_ready(e)
                        tok = None
                        for jj in (0, 1):
                            j = 2 * jh + jj
                            bank = ps_b[2 + jj]
                            if k == 0:
                                bank.acquire(e)
                            bi = e.matmul(bank.ap[0:1, :], onesc_sb.ap,
                                          srcb.ap[:, j * TCW:(j + 1) * TCW],
                                          start=(k == 0), stop=(k == KD - 1))
                            tok = s_t.inc(bi)
                        srcb.consumed(s_t, tok)
                        if k == KD - 1:
                            ps_b[2].produced(s_t, tok)
                            ps_b[3].produced(s_t, tok)
                    T_(fst)

                def fvx(e, jh=jh, phase=phase):
                    ps_b[2].wait_ready(e)
                    ps_b[3].wait_ready(e)
                    if phase == 0 and jh == 0:
                        pool[12].acquire(e)
                    mu = pool[12].ap[0:1, :]
                    tok = None
                    for jj in (0, 1):
                        j = 2 * jh + jj
                        sl = slice(j * TCW, (j + 1) * TCW)
                        if phase == 0:
                            bi = e.tensor_scalar(mu[:, sl],
                                                 ps_b[2 + jj].ap[0:1, :],
                                                 1.0 / 1024.0, None, OP.mult)
                        else:
                            if jh == 0 and jj == 0:
                                pool[10].acquire(e)
                            bi = e.tensor_scalar(pool[10].ap[0:1, sl],
                                                 ps_b[2 + jj].ap[0:1, :],
                                                 1.0 / 1024.0, None, OP.mult)
                        tok = s_v.inc(bi)
                        ps_b[2 + jj].consumed(s_v, tok)
                    if phase == 0 and jh == 1:
                        pool[12].produced(s_v, tok)
                    if phase == 1:
                        hsl = slice(jh * 2 * TCW, (jh + 1) * 2 * TCW)
                        pool[12].wait_ready(e)
                        e.tensor_mul(pool[11].ap[0:1, hsl],
                                     pool[12].ap[0:1, hsl],
                                     pool[12].ap[0:1, hsl])
                        bi = e.tensor_sub(pool[10].ap[0:1, hsl],
                                          pool[10].ap[0:1, hsl],
                                          pool[11].ap[0:1, hsl])
                        tok = s_v.inc(bi)
                        pool[11].consumed(s_v, tok)
                        pool[12].consumed(s_v, tok)
                        pool[10].produced(s_v, tok)
                V_(fvx)
                if phase == 1:
                    def frs(e, jh=jh):
                        pool[10].wait_ready(e)
                        cst_sb.wait_ready(e)
                        hsl = slice(jh * 2 * TCW, (jh + 1) * 2 * TCW)
                        s1 = pool[10].ap[0:1, hsl]
                        e.activation(s1, s1, AF.Ln, bias=cst_sb.ap[0:1, 0:1])
                        if jh == 0:
                            srr.acquire(e)
                        bi = e.activation(srr.ap[:, hsl], s1, AF.Exp,
                                          scale=cst_sb.ap[0:1, 2:3])
                        tok = s_a.inc(bi)
                        pool[10].consumed(s_a, tok)
                        cst_sb.consumed(s_a, tok)
                        srr.produced(s_a, tok)
                    A_(frs)
            yield
        def fmus(e, b=b):
            pool[12].wait_ready(e)
            bi = e.dma_start(out=mu_d[b, 0:1, :],
                             in_=pool[12].ap[0:1, :].bitcast(F32R))
            s = g_st.nxt()
            tok = s.inc(bi, 16)
            pool[12].consumed(s, tok)
            store_toks[("mu", b)] = (s, tok)
            srr.wait_ready(e)
            bi = e.dma_start(out=mu_d[b, 1:2, :], in_=srr.ap)
            s = g_st.nxt()
            tok = s.inc(bi, 16)
            srr.consumed(s, tok)
            store_toks[("rr", b)] = (s, tok)
        G_(fmus)
        for half, key, dstb in ((0, "mu", pool[10]), (1, "rr", pool[12])):
            def fbl(e, half=half, key=key, dstb=dstb, b=b):
                dstb.acquire(e)
                ss, tt = store_toks[(key, b)]
                Buf._wait(e, ss, tt)
                bi = e.dma_start(out=dstb.ap.bitcast(F32R),
                                 in_=mu_d[b, half:half + 1, :]
                                 .broadcast_to([128, T]))
                s = g_st.nxt()
                dstb.produced(s, s.inc(bi, 16))
            G_(fbl)
        yield
        for k in range(KD):
            def fap(e, k=k):
                pool[10].wait_ready(e)
                pool[12].wait_ready(e)
                pool[11].acquire(e)
                e.tensor_sub(pool[11].ap, pool[k].ap, pool[10].ap)
                bi = e.tensor_mul(pool[k].ap, pool[11].ap, pool[12].ap)
                tok = s_v.inc(bi)
                pool[10].consumed(s_v, tok)
                pool[12].consumed(s_v, tok)
                pool[11].consumed(s_v, tok)
                pool[k].produced(s_v, tok)
            V_(fap)
            store(pool[k], xn_d[b, k])
            yield

    def ffn_stage(b, w_ap, in_d, inname, out_d, outname, is_last, in_r):
        pcnt = [0]
        for j in range(NTC):
            for kh in range(2):
                def fld(e, j=j, kh=kh):
                    ld_r[kh].acquire(e)
                    for q in range(4):
                        k = 4 * kh + q
                        rawkey = (inname, b, k, j)
                        if rawkey in store_toks:
                            ss, tt = store_toks[rawkey]
                            Buf._wait(e, ss, tt)
                        bi = e.dma_start(
                            out=ld_r[kh].ap[:, q * TCW:(q + 1) * TCW],
                            in_=in_d[b, k, :, j * TCW:(j + 1) * TCW])
                        s = (g_ld if in_r else g_st).nxt()
                        ld_r[kh].produced(s, s.inc(bi, 16))
                if in_r:
                    L_(fld)
                else:
                    G_(fld)
            for m in range(KD):
                wslot = wt_f[m % 2]

                def fw(e, m=m, wslot=wslot):
                    wslot.acquire(e)
                    bi = e.dma_start(out=wslot.ap, in_=w_ap[m])
                    s = g_ld.nxt()
                    wslot.produced(s, s.inc(bi, 16))
                L_(fw)

                bank = ps_b[pcnt[0] % 2]
                pcnt[0] += 1

                def ftk(e, m=m, wslot=wslot, bank=bank):
                    wslot.wait_ready(e)
                    ld_r[0].wait_ready(e)
                    ld_r[1].wait_ready(e)
                    bank.acquire(e)
                    tok = None
                    for k in range(KD):
                        bi = e.matmul(bank.ap,
                                      wslot.ap[:, k * 128:(k + 1) * 128],
                                      ld_r[k // 4].ap[:, (k % 4) * TCW:
                                                      (k % 4 + 1) * TCW],
                                      start=(k == 0), stop=(k == KD - 1))
                        tok = s_t.inc(bi)
                    wslot.consumed(s_t, tok)
                    ld_r[0].consumed(s_t, tok)
                    ld_r[1].consumed(s_t, tok)
                    bank.produced(s_t, tok)
                T_(ftk)

                slot = pool[8 + m % 2]

                def fe(e, m=m, slot=slot, bank=bank):
                    slot.acquire(e)
                    bank.wait_ready(e)
                    if is_last:
                        ffnb_sb.wait_ready(e)
                        bi = e.activation(slot.ap[:, 0:TCW], bank.ap, AF.Tanh,
                                          bias=ffnb_sb.ap[:, m:m + 1])
                        tok = s_a.inc(bi)
                        ffnb_sb.consumed(s_a, tok)
                    else:
                        bi = e.activation(slot.ap[:, 0:TCW].bitcast(F32R),
                                          bank.ap, AF.Copy)
                        tok = s_a.inc(bi)
                    bank.consumed(s_a, tok)
                    slot.produced(s_a, tok)
                A_(fe)
                store(slot, out_d[b, m, :, j * TCW:(j + 1) * TCW],
                      src_ap=(slot.ap[:, 0:TCW] if is_last
                              else slot.ap[:, 0:TCW].bitcast(F32R)),
                      key=(outname, b, m, j))
                yield

    def ffn(b):
        loads_barrier()
        yield from ffn_stage(b, ffnw, xn_d, "xn", h1_d, "h1", False, False)
        yield from ffn_stage(b, ffna1, h1_d, "h1", h2_d, "h2", False, True)
        yield from ffn_stage(b, ffna2, h2_d, "h2", ffn_d, "ffn", True, True)

    def load_layer_weights(l):
        load(xpwT_sb, W[f"xpwT{l}"])
        load(convw_sb, W[f"convw{l}"])
        load(convb_sb, W[f"convb{l}"])
        load(dtb_sb, W[f"dtb{l}"])
        load(dsk_sb, W[f"dsk{l}"])
        load(lnw_sb, W[f"lnw{l}"])
        load(lnb_sb, W[f"lnb{l}"])
        load(A_sb, W[f"alog{l}"])

        def fa(e):
            A_sb.wait_ready(e)
            bi = e.activation(A_sb.ap, A_sb.ap, AF.Exp)
            A_sb.produced(s_a, s_a.inc(bi))
        A_(fa)

        def fv(e):
            A_sb.wait_ready(e)
            bi = e.tensor_scalar(A_sb.ap, A_sb.ap, -1.0, None, OP.mult)
            A_sb.produced(s_v, s_v.inc(bi))
        V_(fv)

    def gen_A(l, b, stream_d):
        loads_barrier()
        bxc = Buf(pool[11].ap[:, 0:515])
        bcv = Buf(pool[11].ap[:, 1024:1536])
        buz = Buf(pool[11].ap[:, 1536:2048])
        pool[11].cons = {}
        jm = [0]
        for j in range(NTC):
            for kh in range(2):
                def fls(e, j=j, kh=kh):
                    pool[kh].acquire(e)
                    for q in range(4):
                        bi = e.dma_start(
                            out=pool[kh].ap[:, q * TCW:(q + 1) * TCW],
                            in_=stream_d[b, 4 * kh + q, :,
                                         j * TCW:(j + 1) * TCW])
                        s = g_ld.nxt()
                        pool[kh].produced(s, s.inc(bi, 16))
                L_(fls)
            for m in range(2 * KI):
                dtile = m % KI
                is_z = m >= KI
                wslot = wt_in[m % 2]

                def fw(e, mm_=None):
                    wslot_ = wt_in[mm_ % 2]
                    wslot_.acquire(e)
                    bi = e.dma_start(out=wslot_.ap, in_=W[f"inwT{l}"][mm_])
                    s = g_ld.nxt()
                    wslot_.produced(s, s.inc(bi, 16))
                if m == 0:
                    L_(lambda e: fw(e, mm_=0))
                if m + 1 < 2 * KI:
                    L_(lambda e, m=m: fw(e, mm_=m + 1))

                bank = ps_b[3]
                jm[0] += 1

                def ft(e, j=j, wslot=wslot, bank=bank):
                    wslot.wait_ready(e)
                    pool[0].wait_ready(e)
                    pool[1].wait_ready(e)
                    bank.acquire(e)
                    tok = None
                    for k in range(KD):
                        bi = e.matmul(bank.ap,
                                      wslot.ap[:, k * 128:(k + 1) * 128],
                                      pool[k // 4].ap[:, (k % 4) * TCW:
                                                      (k % 4 + 1) * TCW],
                                      start=(k == 0), stop=(k == KD - 1))
                        tok = s_t.inc(bi)
                    wslot.consumed(s_t, tok)
                    pool[0].consumed(s_t, tok)
                    pool[1].consumed(s_t, tok)
                    bank.produced(s_t, tok)
                T_(ft)

                if not is_z:
                    def fe(e, bank=bank, dtile=dtile):
                        bxc.acquire(e)
                        bank.wait_ready(e)
                        hist_sb.wait_ready(e)
                        bi = e.activation(pool[11].ap[:, 3:3 + TCW],
                                          bank.ap, AF.Copy)
                        tok = s_a.inc(bi)
                        bank.consumed(s_a, tok)
                        bxc.produced(s_a, tok)
                    A_(fe)

                    def fcv(e, dtile=dtile, j=j):
                        bxc.wait_ready(e)
                        convw_sb.wait_ready(e)
                        bcv.acquire(e)
                        if j == 0:
                            e.memset(pool[11].ap[:, 0:3], 0.0)
                        else:
                            hist_sb.wait_ready(e)
                            e.tensor_copy(pool[11].ap[:, 0:3],
                                          hist_sb.ap[:, dtile * 4:dtile * 4 + 3])
                        e.tensor_scalar(bcv.ap, pool[11].ap[:, 3:3 + TCW],
                                        convw_sb.ap[:, dtile * 4 + 3:dtile * 4 + 4],
                                        None, OP.mult)
                        for tap in (0, 1, 2):
                            sh = 3 - tap
                            bi = e.scalar_tensor_tensor(
                                bcv.ap, pool[11].ap[:, 3 - sh:3 - sh + TCW],
                                convw_sb.ap[:, dtile * 4 + tap:dtile * 4 + tap + 1],
                                bcv.ap, OP.mult, OP.add)
                        hist_sb.acquire(e)
                        bi2 = e.tensor_copy(hist_sb.ap[:, dtile * 4:dtile * 4 + 3],
                                            pool[11].ap[:, TCW:TCW + 3])
                        tok = s_v.inc(bi2)
                        bxc.consumed(s_v, tok)
                        convw_sb.consumed(s_v, tok)
                        bcv.produced(s_v, tok)
                        hist_sb.produced(s_v, tok)
                    V_(fcv)

                    def fu(e, dtile=dtile, j=j):
                        bcv.wait_ready(e)
                        convb_sb.wait_ready(e)
                        buz.acquire(e)
                        bi = e.activation(buz.ap, bcv.ap, AF.Silu,
                                          bias=convb_sb.ap[:, dtile:dtile + 1])
                        tok = s_a.inc(bi)
                        buz.produced(s_a, tok)
                        u_r.acquire(e)
                        bi = e.activation(u_r.ap[:, j * TCW:(j + 1) * TCW],
                                          bcv.ap, AF.Silu,
                                          bias=convb_sb.ap[:, dtile:dtile + 1])
                        tok = s_a.inc(bi)
                        bcv.consumed(s_a, tok)
                        convb_sb.consumed(s_a, tok)
                        u_r.produced(s_a, tok)
                    A_(fu)
                    store(buz, u_d[b, dtile, :, j * TCW:(j + 1) * TCW])

                    def fxp(e, dtile=dtile, j=j):
                        u_r.wait_ready(e)
                        xpwT_sb.wait_ready(e)
                        if dtile == 0 and j == 0:
                            ps_a.acquire(e)
                        bi = e.matmul(ps_a.ap[0:96, j * TCW:(j + 1) * TCW],
                                      xpwT_sb.ap[:, dtile * 96:(dtile + 1) * 96],
                                      u_r.ap[:, j * TCW:(j + 1) * TCW],
                                      start=(dtile == 0), stop=(dtile == KI - 1))
                        tok = s_t.inc(bi)
                        u_r.consumed(s_t, tok)
                        xpwT_sb.consumed(s_t, tok)
                        if dtile == KI - 1 and j == NTC - 1:
                            ps_a.produced(s_t, tok)
                    T_(fxp)
                else:
                    def fz(e, bank=bank, dtile=dtile, j=j):
                        buz.acquire(e)
                        bank.wait_ready(e)
                        bi = e.activation(buz.ap, bank.ap, AF.Silu)
                        tok = s_a.inc(bi)
                        bank.consumed(s_a, tok)
                        buz.produced(s_a, tok)
                    A_(fz)
                    store(buz, zs_d[b, dtile, :, j * TCW:(j + 1) * TCW])
                yield

        def fxe(e, b=b):
            ps_a.wait_ready(e)
            xdbl[b].acquire(e)
            bi = e.activation(xdbl[b].ap, ps_a.ap[0:96, :], AF.Copy)
            tok = s_a.inc(bi)
            ps_a.consumed(s_a, tok)
            xdbl[b].produced(s_a, tok)
        A_(fxe)

        def fxs(e, b=b):
            xdbl[b].wait_ready(e)
            bi = e.dma_start(out=xdbl_d[b], in_=xdbl[b].ap)
            s = g_st.nxt()
            tok = s.inc(bi, 16)
            xdbl[b].consumed(s, tok)
            store_toks[("xdbl", b)] = (s, tok)
        G_(fxs)
        yield

    def gen_scan(l, b):
        loads_barrier()
        for g in range(NGRP):
            for i in range(G):
                dtile = g * G + i
                load(dtw1, W[f"dtwT{l}"][:, dtile * 128:(dtile + 1) * 128])
                for j in range(NTC):
                    bank = ps_b[2]

                    def fdt(e, dtile=dtile, j=j, bank=bank, i=i):
                        dtw1.wait_ready(e)
                        xdbl[b].wait_ready(e)
                        bank.acquire(e)
                        bi = e.matmul(bank.ap, dtw1.ap,
                                      xdbl[b].ap[0:64, j * TCW:(j + 1) * TCW],
                                      start=True, stop=True)
                        tok = s_t.inc(bi)
                        xdbl[b].consumed(s_t, tok)
                        dtw1.consumed(s_t, tok)
                        bank.produced(s_t, tok)
                    T_(fdt)

                    def fspx(e, dtile=dtile, j=j, bank=bank, i=i):
                        if j == 0:
                            pool[4 + i].acquire(e)
                        bank.wait_ready(e)
                        dtb_sb.wait_ready(e)
                        bi = e.activation(pool[4 + i].ap[:, j * TCW:(j + 1) * TCW],
                                          bank.ap, AF.Exp,
                                          bias=dtb_sb.ap[:, dtile:dtile + 1])
                        tok = s_a.inc(bi)
                        bank.consumed(s_a, tok)
                        dtb_sb.consumed(s_a, tok)
                        if j == NTC - 1:
                            bi = e.activation(pool[4 + i].ap, pool[4 + i].ap,
                                              AF.Ln, bias=cst_sb.ap[:, 1:2])
                            tok = s_a.inc(bi)
                            cst_sb.consumed(s_a, tok)
                        pool[4 + i].produced(s_a, tok)
                    A_(fspx)

                uslot = pool[9] if i == 0 else pool[12]
                load(uslot, u_d[b, dtile])

                def fdtu(e, i=i, uslot=uslot):
                    pool[4 + i].wait_ready(e)
                    uslot.wait_ready(e)
                    pool[6 + i].acquire(e)
                    bi = e.tensor_mul(pool[6 + i].ap, pool[4 + i].ap, uslot.ap)
                    tok = s_v.inc(bi)
                    pool[4 + i].consumed(s_v, tok)
                    uslot.consumed(s_v, tok)
                    pool[6 + i].produced(s_v, tok)
                V_(fdtu)
                yield

            def em_bload(n, half):
                dstb = ld_r[half]
                row = (64 if half == 0 else 80) + n
                def fbl(e, dstb=dstb, row=row):
                    dstb.acquire(e)
                    ss, tt = store_toks[("xdbl", b)]
                    Buf._wait(e, ss, tt)
                    bi = e.dma_start(out=dstb.ap,
                                     in_=xdbl_d[b, row:row + 1, :]
                                     .broadcast_to([128, T]))
                    s = g_ld.nxt()
                    dstb.produced(s, s.inc(bi, 16))
                L_(fbl)

            def em_da(n, i):
                dtile = g * G + i
                ci = dtile * NST + n
                def fda(e, i=i, ci=ci):
                    pool[8].acquire(e)
                    pool[4 + i].wait_ready(e)
                    A_sb.wait_ready(e)
                    bi = e.activation(pool[8].ap, pool[4 + i].ap, AF.Exp,
                                      scale=A_sb.ap[:, ci:ci + 1])
                    tok = s_a.inc(bi)
                    pool[4 + i].consumed(s_a, tok)
                    A_sb.consumed(s_a, tok)
                    pool[8].produced(s_a, tok)
                A_(fda)

            def em_dve(n, i):
                pslot = pool[9] if (n * G + i) % 2 == 0 else pool[12]
                def fscan(e, i=i, n=n, pslot=pslot):
                    pslot.acquire(e)
                    ld_r[0].wait_ready(e)
                    pool[6 + i].wait_ready(e)
                    Bb32 = ld_r[0].ap.bitcast(F32)
                    bi = e.tensor_mul(pslot.ap, pool[6 + i].ap, Bb32)
                    tok = s_v.inc(bi)
                    ld_r[0].consumed(s_v, tok)
                    pool[6 + i].consumed(s_v, tok)
                    pslot.produced(s_v, tok)
                    pool[8].wait_ready(e)
                    pool[10].acquire(e)
                    bi = e.tensor_tensor_scan(pool[10].ap, pool[8].ap,
                                              pslot.ap, 0.0, OP.mult, OP.add)
                    tok = s_v.inc(bi)
                    pool[8].consumed(s_v, tok)
                    pslot.consumed(s_v, tok)
                    pool[10].produced(s_v, tok)
                    ld_r[1].wait_ready(e)
                    Cb32 = ld_r[1].ap.bitcast(F32)
                    if n == 0:
                        pool[2 + i].acquire(e)
                        bi = e.tensor_mul(pool[2 + i].ap, pool[10].ap, Cb32)
                        tok = s_v.inc(bi)
                        pool[2 + i].produced(s_v, tok)
                    else:
                        bi = e.tensor_mul(pslot.ap, pool[10].ap, Cb32)
                        tok = s_v.inc(bi)
                        pslot.produced(s_v, tok)
                    ld_r[1].consumed(s_v, tok)
                    pool[10].consumed(s_v, tok)
                V_(fscan)
                if n > 0:
                    def fadd(e, i=i, pslot=pslot):
                        pslot.wait_ready(e)
                        pool[2 + i].wait_ready(e)
                        bi = e.tensor_add(pool[2 + i].ap, pool[2 + i].ap,
                                          pslot.ap)
                        tok = s_v.inc(bi)
                        pslot.consumed(s_v, tok)
                        pool[2 + i].produced(s_v, tok)
                    V_(fadd)

            em_bload(0, 0)
            em_bload(0, 1)
            em_da(0, 0)
            for n in range(NST):
                em_dve(n, 0)
                em_da(n, 1)
                em_dve(n, 1)
                if n + 1 < NST:
                    em_bload(n + 1, 0)
                    em_da(n + 1, 0)
                    em_bload(n + 1, 1)
                yield

            for i in range(G):
                dtile = g * G + i
                zslot = pool[9]
                uslot = pool[12]
                load(zslot, zs_d[b, dtile])
                load(uslot, u_d[b, dtile])

                def fg(e, i=i, dtile=dtile, zslot=zslot, uslot=uslot):
                    pool[8].acquire(e)
                    uslot.wait_ready(e)
                    pool[2 + i].wait_ready(e)
                    dsk_sb.wait_ready(e)
                    bi = e.scalar_tensor_tensor(
                        pool[8].ap, uslot.ap,
                        dsk_sb.ap[:, dtile:dtile + 1], pool[2 + i].ap,
                        OP.mult, OP.add)
                    tok = s_v.inc(bi)
                    uslot.consumed(s_v, tok)
                    pool[2 + i].consumed(s_v, tok)
                    dsk_sb.consumed(s_v, tok)
                    pool[8].produced(s_v, tok)
                    zslot.wait_ready(e)
                    pool[10].acquire(e)
                    bi = e.tensor_mul(pool[10].ap.bitcast(F32R), pool[8].ap,
                                      zslot.ap)
                    tok = s_v.inc(bi)
                    pool[8].consumed(s_v, tok)
                    zslot.consumed(s_v, tok)
                    pool[10].produced(s_v, tok)
                V_(fg)
                store(pool[10], yp_d[b, dtile],
                      src_ap=pool[10].ap.bitcast(F32R))
                yield

    def gen_out_mm(l, b):
        loads_barrier()
        for m in range(KD):
            wslot = wt_out[m % 2]

            def fw(e, m=m, wslot=wslot):
                wslot.acquire(e)
                bi = e.dma_start(out=wslot.ap, in_=W[f"outwT{l}"][m])
                s = g_ld.nxt()
                wslot.produced(s, s.inc(bi, 16))
            L_(fw)

            for k in range(KI):
                load(u_r, yp_d[b, k])

                def ftk(e, k=k, wslot=wslot):
                    wslot.wait_ready(e)
                    u_r.wait_ready(e)
                    if k == 0:
                        ps_a.acquire(e)
                    tok = None
                    for j in range(NTC):
                        bi = e.matmul(ps_a.ap[:, j * TCW:(j + 1) * TCW],
                                      wslot.ap[:, k * 128:(k + 1) * 128],
                                      u_r.ap[:, j * TCW:(j + 1) * TCW],
                                      start=(k == 0), stop=(k == KI - 1))
                        tok = s_t.inc(bi)
                    u_r.consumed(s_t, tok)
                    wslot.consumed(s_t, tok)
                    if k == KI - 1:
                        ps_a.produced(s_t, tok)
                T_(ftk)
                yield

            def fe(e, m=m):
                pool[11].acquire(e)
                ps_a.wait_ready(e)
                bi = e.activation(pool[11].ap.bitcast(F32R), ps_a.ap, AF.Copy)
                tok = s_a.inc(bi)
                ps_a.consumed(s_a, tok)
                pool[11].produced(s_a, tok)
            A_(fe)
            store(pool[11], h1_d[b, m], src_ap=pool[11].ap.bitcast(F32R),
                  key=("o", b, m))
            yield

    def gen_ln(l, b, stream_d, next_d):
        otiles = [pool[2 + kk] for kk in range(KD)]
        for k in range(KD):
            def flo(e, k=k):
                otiles[k].acquire(e)
                if ("o", b, k) in store_toks:
                    ss, tt = store_toks[("o", b, k)]
                    Buf._wait(e, ss, tt)
                bi = e.dma_start(out=otiles[k].ap.bitcast(F32R),
                                 in_=h1_d[b, k])
                s = g_ld.nxt()
                otiles[k].produced(s, s.inc(bi, 16))
            L_(flo)
        smu_ap = ld_r[0].ap[0:1, :]
        def fvinit(e):
            ld_r[0].acquire(e)
            pool[12].acquire(e)
        V_(fvinit)
        for phase in (0, 1):
            for jh in (0, 1):
                for k in range(KD):
                    if phase == 1:
                        def fsq(e, k=k):
                            pool[10].acquire(e)
                            otiles[k].wait_ready(e)
                            bi = e.activation(pool[10].ap, otiles[k].ap,
                                              AF.Square)
                            tok = s_a.inc(bi)
                            pool[10].produced(s_a, tok)
                            otiles[k].consumed(s_a, tok)
                        A_(fsq)

                    def fst(e, k=k, jh=jh, phase=phase):
                        srcb = otiles[k] if phase == 0 else pool[10]
                        srcb.wait_ready(e)
                        tok = None
                        for jj in (0, 1):
                            j = 2 * jh + jj
                            if k == 0:
                                ps_b[jj].acquire(e)
                            bi = e.matmul(ps_b[jj].ap[0:1, :], onesc_sb.ap,
                                          srcb.ap[:, j * TCW:(j + 1) * TCW],
                                          start=(k == 0), stop=(k == KD - 1))
                            tok = s_t.inc(bi)
                        srcb.consumed(s_t, tok)
                        if k == KD - 1:
                            ps_b[0].produced(s_t, tok)
                            ps_b[1].produced(s_t, tok)
                    T_(fst)

                def fvx(e, jh=jh, phase=phase):
                    ps_b[0].wait_ready(e)
                    ps_b[1].wait_ready(e)
                    mu2 = pool[12].ap[0:1, :]
                    tok = None
                    for jj in (0, 1):
                        j = 2 * jh + jj
                        sl = slice(j * TCW, (j + 1) * TCW)
                        if phase == 0:
                            e.tensor_scalar(smu_ap[:, sl], ps_b[jj].ap[0:1, :],
                                            1.0 / 1024.0, None, OP.mult)
                            bi = e.tensor_scalar(mu2[:, sl],
                                                 ps_b[jj].ap[0:1, :],
                                                 1.0 / 1024.0, None, OP.mult)
                        else:
                            if jj == 0:
                                pool[10].acquire(e)
                            bi = e.tensor_scalar(pool[10].ap[0:1, sl],
                                                 ps_b[jj].ap[0:1, :],
                                                 1.0 / 1024.0, None, OP.mult)
                        tok = s_v.inc(bi)
                        ps_b[jj].consumed(s_v, tok)
                    if phase == 0 and jh == 1:
                        bi = e.tensor_mul(smu_ap, mu2, mu2)
                        tok = s_v.inc(bi)
                        ld_r[0].produced(s_v, tok)
                        pool[12].produced(s_v, tok)
                    if phase == 1:
                        hsl = slice(jh * 2 * TCW, (jh + 1) * 2 * TCW)
                        ld_r[0].wait_ready(e)
                        bi = e.tensor_sub(pool[10].ap[0:1, hsl],
                                          pool[10].ap[0:1, hsl],
                                          smu_ap[:, hsl].bitcast(F32))
                        tok = s_v.inc(bi)
                        ld_r[0].consumed(s_v, tok)
                        pool[10].produced(s_v, tok)
                V_(fvx)
                if phase == 1:
                    def frs(e, jh=jh):
                        pool[10].wait_ready(e)
                        cst_sb.wait_ready(e)
                        hsl = slice(jh * 2 * TCW, (jh + 1) * 2 * TCW)
                        s1 = pool[10].ap[0:1, hsl]
                        e.activation(s1, s1, AF.Ln, bias=cst_sb.ap[0:1, 0:1])
                        if jh == 0:
                            srr.acquire(e)
                        bi = e.activation(srr.ap[:, hsl], s1, AF.Exp,
                                          scale=cst_sb.ap[0:1, 2:3])
                        tok = s_a.inc(bi)
                        pool[10].consumed(s_a, tok)
                        cst_sb.consumed(s_a, tok)
                        srr.produced(s_a, tok)
                    A_(frs)
            yield
        def fmus(e):
            pool[12].wait_ready(e)
            bi = e.dma_start(out=mu_d[b, 0:1, :],
                             in_=pool[12].ap[0:1, :].bitcast(F32R))
            s = g_st.nxt()
            tok = s.inc(bi, 16)
            pool[12].consumed(s, tok)
            store_toks[("mu", b)] = (s, tok)
            srr.wait_ready(e)
            bi = e.dma_start(out=mu_d[b, 1:2, :], in_=srr.ap)
            s = g_st.nxt()
            tok = s.inc(bi, 16)
            srr.consumed(s, tok)
            store_toks[("rr", b)] = (s, tok)
        G_(fmus)

        for half, key in ((0, "mu"), (1, "rr")):
            def fbl(e, half=half, key=key):
                ld_r[half].acquire(e)
                ss, tt = store_toks[(key, b)]
                Buf._wait(e, ss, tt)
                bi = e.dma_start(out=ld_r[half].ap,
                                 in_=mu_d[b, half:half + 1, :]
                                 .broadcast_to([128, T]))
                s = g_st.nxt()
                ld_r[half].produced(s, s.inc(bi, 16))
            G_(fbl)
        yield

        for k in range(KD):
            def fap(e, k=k):
                ld_r[0].wait_ready(e)
                ld_r[1].wait_ready(e)
                if k == 0:
                    pool[10].acquire(e)
                lnw_sb.wait_ready(e)
                lnb_sb.wait_ready(e)
                mub = ld_r[0].ap.bitcast(F32)
                rrb = ld_r[1].ap.bitcast(F32)
                e.tensor_sub(pool[10].ap, otiles[k].ap, mub)
                e.tensor_mul(pool[10].ap, pool[10].ap, rrb)
                bi = e.tensor_scalar(otiles[k].ap, pool[10].ap,
                                     lnw_sb.ap[:, k:k + 1],
                                     lnb_sb.ap[:, k:k + 1],
                                     OP.mult, OP.add)
                tok = s_v.inc(bi)
                ld_r[0].consumed(s_v, tok)
                ld_r[1].consumed(s_v, tok)
                lnw_sb.consumed(s_v, tok)
                lnb_sb.consumed(s_v, tok)
                otiles[k].produced(s_v, tok)
            V_(fap)
        yield
        for k in range(KD):
            load(pool[12], stream_d[b, k])

            def fres(e, k=k):
                pool[12].wait_ready(e)
                otiles[k].wait_ready(e)
                pool[10].acquire(e)
                bi = e.tensor_add(pool[10].ap, otiles[k].ap, pool[12].ap)
                tok = s_v.inc(bi)
                pool[12].consumed(s_v, tok)
                otiles[k].consumed(s_v, tok)
                pool[10].produced(s_v, tok)
            V_(fres)
            store(pool[10], next_d[b, k])
            yield

    def run_gen(gen):
        for _ in gen:
            pass

    def interleave(g1, g2):
        done1 = done2 = False
        while not (done1 and done2):
            if not done1:
                try:
                    next(g1)
                except StopIteration:
                    done1 = True
            if not done2:
                try:
                    next(g2)
                except StopIteration:
                    done2 = True

    def final(b):
        loads_barrier()
        for k in range(KD):
            base = (k % 2) * 4
            i0, i1, i2, i3 = base, base + 1, base + 2, base + 3
            load(pool[i0], xfm[b, k])
            load(pool[i1], m2_d[b, k])
            load(pool[i2], ffn_d[b, k])

            def fv(e, i0=i0, i1=i1, i2=i2, i3=i3):
                pool[i0].wait_ready(e)
                pool[i1].wait_ready(e)
                pool[i2].wait_ready(e)
                pool[i3].acquire(e)
                e.tensor_add(pool[i3].ap, pool[i0].ap, pool[i1].ap)
                bi = e.tensor_add(pool[i3].ap, pool[i3].ap, pool[i2].ap)
                tok = s_v.inc(bi)
                pool[i0].consumed(s_v, tok)
                pool[i1].consumed(s_v, tok)
                pool[i2].consumed(s_v, tok)
                pool[i3].produced(s_v, tok)
            V_(fv)
            store(pool[i3], outfm[b, k])

    run_gen(p0_ln(0))
    flush()
    interleave(p0_ln(1), ffn(0))
    flush()
    ffn1_pending = ffn(1)
    _NL = int(_os.environ.get("KD_NLAYERS", "2"))
    pending_ln = None
    for l in range(_NL):
        stream_d = xn_d if l == 0 else m1_d
        next_d = m1_d if l == 0 else m2_d
        load_layer_weights(l)
        flush()
        if pending_ln is not None:
            interleave(pending_ln, gen_A(l, 0, stream_d))
            pending_ln = None
        elif ffn1_pending is not None:
            interleave(ffn1_pending, gen_A(l, 0, stream_d))
            ffn1_pending = None
        else:
            run_gen(gen_A(l, 0, stream_d))
        flush()
        interleave(gen_scan(l, 0), gen_A(l, 1, stream_d))
        flush()
        interleave(gen_scan(l, 1), gen_out_mm(l, 0))
        flush()
        interleave(gen_ln(l, 0, stream_d, next_d), gen_out_mm(l, 1))
        flush()
        pending_ln = gen_ln(l, 1, stream_d, next_d)
    if pending_ln is not None:
        run_gen(pending_ln)
        flush()
    for b in range(BC):
        final(b)
    flush()
    def f_fin_g(e):
        for s in g_st.sems:
            e.wait_ge(s.h, s.n)
    G_(f_fin_g)
    def f_fin_l(e):
        for s in g_ld.sems:
            e.wait_ge(s.h, s.n)
    L_(f_fin_l)
    def f_fin_a(e):
        for s in g_sc.sems:
            e.wait_ge(s.h, s.n)
    A_(f_fin_a)

    flush()

    st.close()
    nc._kd_trace = trace_log
    return nc


def _ffn_tile(w):
    w = np.asarray(w).reshape(8, 128, 8, 128)
    return np.ascontiguousarray(w.transpose(2, 1, 0, 3)).reshape(8, 128, 1024)


def host_inputs(inputs, core):
    asc = np.ascontiguousarray
    x = np.asarray(inputs["L0_attn_x"])[2 * core:2 * core + 2]
    d = {
        "xfm": asc(x.transpose(0, 2, 1)).reshape(BC, KD, 128, T),
        "ffnw": _ffn_tile(inputs["L0_ffn_weight"]),
        "ffna1": _ffn_tile(inputs["L0_ffn_aux_weight1"]),
        "ffna2": _ffn_tile(inputs["L0_ffn_aux_weight2"]),
        "ffnb": asc(np.asarray(inputs["L0_ffn_bias"]).reshape(8, 128).T),
        "onesr": np.ones((1, 128), np.float32),
        "onesc": np.ones((128, 1), np.float32),
        "cst": np.repeat(np.array([[EPS, 1.0, -0.5, 0.0]], np.float32), 128, axis=0),
    }
    for l in (0, 1):
        inwT = np.asarray(inputs["L0_attn_in_proj_w"])[l].T
        d[f"inwT{l}"] = asc(inwT.reshape(8, 128, 32, 128)
                            .transpose(2, 1, 0, 3)).reshape(32, 128, 8 * 128)
        outwT = np.asarray(inputs["L0_attn_out_proj_w"])[l].T
        d[f"outwT{l}"] = asc(outwT.reshape(16, 128, 8, 128)
                             .transpose(2, 1, 0, 3)).reshape(8, 128, 16 * 128)
        d[f"dtwT{l}"] = asc(np.asarray(inputs["L0_attn_dt_proj_w"])[l].T)
        d[f"xpwT{l}"] = asc(np.asarray(inputs["L0_attn_x_proj_w"])[l].T
                            .reshape(16, 128, 96).transpose(1, 0, 2)).reshape(128, 16 * 96)
        d[f"convw{l}"] = asc(np.asarray(inputs["L0_attn_conv_w"])[l]
                             .reshape(16, 128, 4).transpose(1, 0, 2)).reshape(128, 64)
        d[f"convb{l}"] = asc(np.asarray(inputs["L0_attn_conv_b"])[l].reshape(16, 128).T)
        d[f"dtb{l}"] = asc(np.asarray(inputs["L0_attn_dt_proj_b"])[l].reshape(16, 128).T)
        d[f"dsk{l}"] = asc(np.asarray(inputs["L0_attn_Dskip"])[l].reshape(16, 128).T)
        d[f"alog{l}"] = asc(np.asarray(inputs["L0_attn_A_log"])[l]
                            .reshape(16, 128, 16).transpose(1, 0, 2)).reshape(128, 256)
        d[f"lnw{l}"] = asc(np.asarray(inputs["L0_attn_ln_w"])[l].reshape(8, 128).T)
        d[f"lnb{l}"] = asc(np.asarray(inputs["L0_attn_ln_b"])[l].reshape(8, 128).T)
    return {k: np.ascontiguousarray(v, dtype=np.float32) for k, v in d.items()}


def kernel(**inputs):
    nc = build_nc()
    in_maps = [host_inputs(inputs, c) for c in range(NCORE)]
    res = run_bass_kernel_spmd(nc, in_maps, core_ids=list(range(NCORE)))
    outs = []
    for c in range(NCORE):
        o = np.asarray(res.results[c]["outfm"])
        o = o.reshape(BC, 1024, T).transpose(0, 2, 1)
        outs.append(o)
    return np.ascontiguousarray(np.concatenate(outs, axis=0), dtype=np.float32)
